# version 12
# baseline (speedup 1.0000x reference)
import numpy as np
import ml_dtypes
from contextlib import ExitStack

import concourse.bass as bass
import concourse.mybir as mybir
from concourse.bass_utils import run_bass_kernel_spmd

F32 = mybir.dt.float32
BF16 = mybir.dt.bfloat16
AF = mybir.ActivationFunctionType
ALU = mybir.AluOpType
AX = mybir.AxisListType.X

NCORE = 8
T = 1024
NLT = 8
LC = 128
NCH = 8
HID = 2048
INTER = 4096
NHB = 35
H = 64
PD = 64
NS = 128
FFH = 5632
NHT = 44
EPS = 1e-5
MAGIC = 12582912.0
WCNT = float(5632 * 2048)

_DEBUG_WANT = set()


def _prep_shared(ins):
    W = np.ascontiguousarray(ins["L0_attn_in_proj_w"], dtype=np.float32)
    b_in = np.zeros(8576, np.float32)
    b_in[:8512] = np.asarray(ins["L0_attn_in_proj_b"], np.float32)
    Wh = np.zeros((2048, NHB * 128), np.float32)
    Wh[:, :4416] = W[:, 4096:8512]
    d = {}
    d["w_hbc"] = np.ascontiguousarray(
        Wh.reshape(16, 128, NHB, 128).transpose(2, 1, 0, 3))
    d["w_gate"] = np.ascontiguousarray(
        W[:, :4096].reshape(16, 128, 8, 512).transpose(2, 1, 0, 3))
    d["b_cols"] = np.ascontiguousarray(b_in.reshape(67, 128).T)
    d["b_gate_row"] = np.ascontiguousarray(b_in[None, :4096])
    d["convw"] = np.ascontiguousarray(
        np.asarray(ins["L0_attn_conv_w"], np.float32).reshape(34, 128, 4)
        .transpose(1, 0, 2).reshape(128, 136))
    d["dtb"] = np.asarray(ins["L0_attn_dt_bias"], np.float32).reshape(64, 1).copy()
    d["alog"] = np.asarray(ins["L0_attn_A_log"], np.float32).reshape(64, 1).copy()
    d["d_row"] = np.asarray(ins["L0_attn_D"], np.float32).reshape(1, 64).copy()
    d["gam"] = np.ascontiguousarray(
        np.asarray(ins["L0_attn_norm_weight"], np.float32).reshape(32, 128).T)
    Wo = np.ascontiguousarray(ins["L0_attn_out_proj_w"], dtype=np.float32)
    d["w_out"] = np.ascontiguousarray(
        Wo.reshape(32, 128, 16, 128).transpose(2, 1, 0, 3))
    d["b_out_cols"] = np.ascontiguousarray(
        np.asarray(ins["L0_attn_out_proj_b"], np.float32).reshape(16, 128).T)
    w1 = np.ascontiguousarray(ins["L0_ffn_w1"], dtype=np.float32)
    w2 = np.ascontiguousarray(ins["L0_ffn_w2"], dtype=np.float32)
    w3 = np.ascontiguousarray(ins["L0_ffn_w3"], dtype=np.float32)
    d["w1t"] = np.ascontiguousarray(w1.reshape(NHT, 128, 16, 128).transpose(0, 3, 2, 1))
    d["w3t"] = np.ascontiguousarray(w3.reshape(NHT, 128, 16, 128).transpose(0, 3, 2, 1))
    d["w2t"] = np.ascontiguousarray(w2.reshape(16, 128, NHT, 128).transpose(0, 3, 2, 1))
    d["identf"] = np.eye(128, dtype=np.float32)
    d["identb"] = np.eye(128, dtype=ml_dtypes.bfloat16)
    d["ones_row"] = np.ones((1, 128), np.float32)
    d["ones_col"] = np.ones((128, 1), np.float32)
    d["trimask"] = np.triu(np.ones((128, 128), np.float32))
    d["_w1"], d["_w2"], d["_w3"] = w1, w2, w3
    return d


def _prep_core(ins, shared, c):
    x_full = np.ascontiguousarray(np.asarray(ins["L0_attn_hidden_states"])[0],
                                  dtype=np.float32)
    lo = c * T
    d = {k: v for k, v in shared.items() if not k.startswith("_")}
    d["x"] = np.ascontiguousarray(x_full[lo:lo + T])
    if c == 0:
        d["halo_x"] = np.zeros((3, HID), np.float32)
        d["halo_mask"] = np.zeros((128, 1), np.float32)
    else:
        d["halo_x"] = np.ascontiguousarray(x_full[lo - 3:lo])
        d["halo_mask"] = np.ones((128, 1), np.float32)
    sm = np.zeros((128, 7), np.float32)
    sm[:, :c] = 1.0
    d["smask"] = sm
    for nm, w in (("ws1", shared["_w1"]), ("ws2", shared["_w2"]), ("ws3", shared["_w3"])):
        fl = w.reshape(-1)
        n = fl.size // NCORE
        d[nm] = np.ascontiguousarray(fl[c * n:(c + 1) * n].reshape(128, n // 128))
    return d


class Sem:
    def __init__(self, h):
        self.h, self.n = h, 0

    def bump(self, instr, k=1):
        instr.then_inc(self.h, k)
        self.n += k
        return self.n

    def dma(self, instr):
        return self.bump(instr, 16)


def build_nc():
    nc = bass.Bass("TRN2", target_bir_lowering=False)
    I = {}

    def inp(name, shape, dt=F32):
        I[name] = nc.dram_tensor(name, list(shape), dt, kind="ExternalInput").ap()

    inp("x", [T, HID]); inp("halo_x", [3, HID]); inp("halo_mask", [128, 1])
    inp("smask", [128, 7])
    inp("w_hbc", [NHB, 128, 16, 128]); inp("w_gate", [8, 128, 16, 512])
    inp("b_cols", [128, 67]); inp("b_gate_row", [1, 4096]); inp("convw", [128, 136])
    inp("dtb", [64, 1]); inp("alog", [64, 1]); inp("d_row", [1, 64])
    inp("gam", [128, 32]); inp("w_out", [16, 128, 32, 128]); inp("b_out_cols", [128, 16])
    inp("w1t", [NHT, 128, 16, 128]); inp("w3t", [NHT, 128, 16, 128])
    inp("w2t", [16, 128, NHT, 128])
    inp("ws1", [128, 11264]); inp("ws2", [128, 11264]); inp("ws3", [128, 11264])
    inp("identf", [128, 128]); inp("identb", [128, 128], BF16)
    inp("ones_row", [1, 128]); inp("ones_col", [128, 1])
    inp("trimask", [128, 128])

    out_h = nc.dram_tensor("out", [T, HID], F32, kind="ExternalOutput").ap()
    cc_in = nc.dram_tensor("cc_in", [128, 4164], F32, kind="Internal").ap()
    cc_out = nc.dram_tensor("cc_out", [NCORE * 128, 4164], F32, kind="Internal").ap()
    th1 = nc.dram_tensor("th1", [NHT, 128, 2048], BF16, kind="Internal").ap()
    th3 = nc.dram_tensor("th3", [NHT, 128, 2048], BF16, kind="Internal").ap()
    th2 = nc.dram_tensor("th2", [16, 128, FFH], BF16, kind="Internal").ap()

    DBG = {}

    def dbgt(name, shape, dt=F32):
        if name in _DEBUG_WANT:
            DBG[name] = nc.dram_tensor("dbg_" + name, list(shape), dt,
                                       kind="ExternalOutput").ap()
            return DBG[name]
        return None

    ctx = ExitStack()
    with ctx:
        def sb(name, shape, dt=F32, st=None):
            return ctx.enter_context(nc.sbuf_tensor("sb_" + name, shape, dt))

        def psum(name, shape):
            return ctx.enter_context(nc.psum_tensor(name, shape, F32))

        pA0 = psum("pA0", [128, 512]); pA1 = psum("pA1", [128, 512])
        pB0 = psum("pB0", [128, 512]); pB1 = psum("pB1", [128, 512])
        pbig = psum("pbig", [128, 1024])
        pD = psum("pD", [128, 512]); pE = psum("pE", [128, 512])
        pbig_b = pbig[:].bitcast(BF16)
        pE_b = pE[:].bitcast(BF16)

        sLd = Sem(ctx.enter_context(nc.semaphore("sLd")))
        sLdA = Sem(ctx.enter_context(nc.semaphore("sLdA")))
        sLdB = Sem(ctx.enter_context(nc.semaphore("sLdB")))
        sCastA = Sem(ctx.enter_context(nc.semaphore("sCastA")))
        sCastB = Sem(ctx.enter_context(nc.semaphore("sCastB")))
        sCastC = Sem(ctx.enter_context(nc.semaphore("sCastC")))
        sSt = Sem(ctx.enter_context(nc.semaphore("sSt")))
        sT = Sem(ctx.enter_context(nc.semaphore("sT")))
        sV = Sem(ctx.enter_context(nc.semaphore("sV")))
        sS = Sem(ctx.enter_context(nc.semaphore("sS")))
        sG = Sem(ctx.enter_context(nc.semaphore("sG")))
        sBar = Sem(ctx.enter_context(nc.semaphore("sBar")))
        sCC = Sem(ctx.enter_context(nc.semaphore("sCC")))
        sWs = Sem(ctx.enter_context(nc.semaphore("sWs")))
        sWs2 = Sem(ctx.enter_context(nc.semaphore("sWs2")))

        block = ctx.enter_context(nc.Block())
        ENG = ("tensor", "vector", "scalar", "gpsimd", "sync")

        FENCE_OPS = frozenset((
            "tensor_scalar", "tensor_single_scalar", "tensor_tensor", "tensor_add",
            "tensor_sub", "tensor_mul", "tensor_max", "tensor_copy", "tensor_reduce",
            "reduce_max", "reduce_sum", "scalar_tensor_tensor", "tensor_tensor_scan",
            "memset", "memzero", "activation", "copy", "sign", "mul", "add",
            "square", "sqrt", "select", "copy_predicated", "reciprocal",
            "tensor_scalar_add", "tensor_scalar_sub", "tensor_scalar_mul",
            "tensor_scalar_max", "tensor_scalar_min", "tensor_relu", "iota"))

        class FEng:
            def __init__(self, eng, sem):
                self._eng, self._sem = eng, sem

            def __getattr__(self, name):
                a = getattr(self._eng, name)
                if name in FENCE_OPS:
                    def w(*args, **kw):
                        self._eng.wait_ge(self._sem.h, self._sem.n)
                        r = a(*args, **kw)
                        self._sem.bump(r)
                        return r
                    return w
                return a

            def pad(self, k=1):
                for _ in range(k):
                    self._eng.wait_ge(self._sem.h, self._sem.n)
                    self._sem.bump(self._eng.nop())

        def V(f):
            block.vector(lambda e: f(FEng(e, sV)))

        def S(f):
            block.scalar(lambda e: f(FEng(e, sS)))

        def G(f):
            block.gpsimd(lambda e: f(FEng(e, sG)))

        class TEng:
            def __init__(self, eng):
                self._eng = eng

            def matmul(self, *args, **kw):
                r = self._eng.matmul(*args, **kw)
                if kw.get("stop", True):
                    sT.bump(r)
                return r

            def transpose(self, *args, **kw):
                r = self._eng.transpose(*args, **kw)
                sT.bump(r)
                return r

            def pad(self, k=1):
                for _ in range(k):
                    self._eng.wait_ge(sT.h, sT.n)
                    sT.bump(self._eng.nop())

            def fence(self):
                self._eng.wait_ge(sT.h, sT.n)

            def __getattr__(self, name):
                return getattr(self._eng, name)

        def TE(f):
            block.tensor(lambda e: f(TEng(e)))

        def SY(f):
            block.sync(f)

        ESEM = {"tensor": sT, "vector": sV, "scalar": sS, "gpsimd": sG}

        def barrier():
            for e in ENG:
                def _b(eng, e=e):
                    if e in ESEM:
                        eng.wait_ge(ESEM[e].h, ESEM[e].n)
                    eng.nop().then_inc(sBar.h, 1)
                getattr(block, e)(_b)
            sBar.n += 5
            tgt = sBar.n
            for e in ENG:
                def _w(eng, tgt=tgt):
                    eng.wait_ge(sBar.h, tgt)
                getattr(block, e)(_w)

        def dbg_store(name, src_ap, shape, dt=F32):
            t = dbgt(name, shape, dt)
            if t is None:
                return
            def _st(eng, t=t, src_ap=src_ap):
                sSt.dma(eng.dma_start(out=t, in_=src_ap))
                eng.wait_ge(sSt.h, sSt.n)
            block.sync(_st)

        identb = sb("identb", [128, 128], BF16)
        identf = sb("identf", [128, 128])
        ones_row = sb("ones_row", [1, 128]); ones_col = sb("ones_col", [128, 1])
        trimask = sb("trimask", [128, 128])
        b_cols = sb("b_cols", [128, 67]); mb_cols = sb("mb_cols", [128, 35])
        convw = sb("convw", [128, 136])
        dtb = sb("dtb", [64, 1]); alog = sb("alog", [64, 1]); d_rowt = sb("d_rowt", [1, 64])
        gam = sb("gam", [128, 32]); b_out_cols = sb("b_out_cols", [128, 16])
        halo_mask = sb("halo_mask", [128, 1]); smask = sb("smask", [128, 7])
        A_col = sb("A_col", [64, 1]); dtbias2 = sb("dtbias2", [64, 1])
        eps_col = sb("eps_col", [128, 1])
        ln_m = sb("ln_m", [128, NLT]); ln_r = sb("ln_r", [128, NLT])
        wpart = sb("wpart", [128, 4]); wk = sb("wk", [128, 4])
        xn1hF = sb("xn1hF", [128, 16 * 3], BF16)
        dtraw = sb("dtraw", [64, T])
        B_F = sb("B_F", [128, T], BF16); C_F = sb("C_F", [128, T], BF16)
        B_Tt = sb("B_Tt", [128, NCH * 128], BF16)
        q_m = sb("q_m", [128, NLT]); q_r = sb("q_r", [128, NLT])
        q_am = sb("q_am", [128, NLT]); q_sa = sb("q_sa", [128, NLT])
        akrow = sb("akrow", [1, NCH * 64 + 128])
        dtil = sb("dtil", [128, 64]); Dtile = sb("Dtile", [128, 64])

        NMEGA = 48640
        MEGA = sb("MEGA", [128, NMEGA])
        MEGA_B = MEGA[:].bitcast(BF16)

        def vf(off, cols, rows=128):
            assert off + cols <= NMEGA
            return MEGA[0:rows, off:off + cols]

        def vb(off, cols_f32, rows=128):
            assert off + cols_f32 <= NMEGA
            return MEGA_B[0:rows, 2 * off:2 * (off + cols_f32)]

        OFF_A32, OFF_A, OFF_B, OFF_D = 0, 8192, 24576, 40960
        xn1F = vb(OFF_A32, 8192)
        hsT = vb(OFF_A, 16384)
        xgnF = vb(OFF_A, 16384)
        sgT = vb(OFF_B, 16384)
        x2 = vf(OFF_B, 16384)

        xT = vf(OFF_A, 16384)
        xn1T = vb(OFF_B, 8192)
        sp_scr = vf(OFF_B + 8192, 2048)
        xhT = vf(OFF_B + 10240, 2048, rows=3)
        xn1h = vb(OFF_B + 12288, 1024, rows=3)
        hstat = vf(OFF_B + 13312, 6, rows=3)
        wcol = vf(OFF_B + 13320, 12)
        ln_s = vf(OFF_B + 13336, NLT)
        ln_q = vf(OFF_B + 13344, NLT)
        wsl = [vf(OFF_D, 2816), vf(OFF_D + 2816, 2816)]

        def p0_sync(eng):
            for dst, src in (
                (identb, "identb"), (identf, "identf"), (ones_row, "ones_row"),
                (ones_col, "ones_col"), (trimask, "trimask"),
                (b_cols, "b_cols"), (convw, "convw"),
                (dtb, "dtb"), (alog, "alog"), (d_rowt, "d_row"), (gam, "gam"),
                (b_out_cols, "b_out_cols"), (halo_mask, "halo_mask"),
                (smask, "smask"), (xhT, "halo_x"),
            ):
                sLd.dma(eng.dma_start(out=dst[:], in_=I[src]))
            for lt in range(NLT):
                sLd.dma(eng.dma_start(out=xT[:, lt * HID:(lt + 1) * HID],
                                      in_=I["x"][lt * 128:(lt + 1) * 128, :]))
        SY(p0_sync)
        ld0 = sLd.n

        def p0_vec(eng):
            eng.memset(wpart[:, 3:4], 0.0)
            eng.memset(eps_col[:], EPS)
            eng.wait_ge(sLd.h, ld0)
            eng.tensor_scalar(mb_cols[:], b_cols[:, 32:67],
                              halo_mask[:, 0:1], None, ALU.mult)
        V(p0_vec)

        barrier()

        def p1_scalar(eng):
            for lt in range(NLT):
                xt = xT[:, lt * HID:(lt + 1) * HID]
                eng.activation(sp_scr[:], xt, AF.Identity, accum_out=ln_s[:, lt:lt + 1])
                eng.activation(sp_scr[:], xt, AF.Square, accum_out=ln_q[:, lt:lt + 1])
            eng.activation(sp_scr[0:3, :], xhT[:], AF.Identity, accum_out=hstat[:, 0:1])
            eng.activation(sp_scr[0:3, :], xhT[:], AF.Square, accum_out=hstat[:, 1:2])
            eng.activation(A_col[:], alog[:], AF.Exp)
        S(p1_scalar)
        c_p1s = sS.n

        def p1_vec(eng):
            eng.wait_ge(sS.h, c_p1s)
            eng.tensor_scalar(ln_m[:], ln_s[:], 1.0 / HID, None, ALU.mult)
            eng.tensor_scalar(ln_r[:], ln_q[:], 1.0 / HID, None, ALU.mult)
            eng.tensor_tensor(ln_s[:], ln_m[:], ln_m[:], ALU.mult)
            eng.tensor_sub(ln_r[:], ln_r[:], ln_s[:])
            eng.tensor_scalar(hstat[:, 2:3], hstat[:, 0:1], 1.0 / HID, None, ALU.mult)
            eng.tensor_scalar(hstat[:, 3:4], hstat[:, 1:2], 1.0 / HID, None, ALU.mult)
            eng.tensor_tensor(hstat[:, 4:5], hstat[:, 2:3], hstat[:, 2:3], ALU.mult)
            eng.tensor_sub(hstat[:, 3:4], hstat[:, 3:4], hstat[:, 4:5])
            eng.tensor_scalar(A_col[:], A_col[:], -1.0, None, ALU.mult)
            eng.tensor_scalar(dtbias2[:], dtb[:], 0.0, None, ALU.add)
        V(p1_vec)
        c_p1v = sV.n

        def p1_scalar2(eng):
            eng.wait_ge(sV.h, c_p1v)
            eng.activation(ln_r[:], ln_r[:], AF.Ln, bias=eps_col[:])
            eng.activation(ln_r[:], ln_r[:], AF.Exp, scale=-0.5)
            eng.activation(hstat[:, 3:4], hstat[:, 3:4], AF.Ln, bias=eps_col[0:3, :])
            eng.activation(hstat[:, 3:4], hstat[:, 3:4], AF.Exp, scale=-0.5)
        S(p1_scalar2)
        c_p1s2 = sS.n

        v_xn1_0 = sV.n
        def p1_vec2(eng):
            eng.wait_ge(sS.h, c_p1s2)
            for lt in range(NLT):
                eng.tensor_scalar(
                    xn1T[:, lt * HID:(lt + 1) * HID],
                    xT[:, lt * HID:(lt + 1) * HID],
                    ln_m[:, lt:lt + 1], ln_r[:, lt:lt + 1], ALU.subtract, ALU.mult)
            eng.tensor_scalar(xn1h[:], xhT[:], hstat[:, 2:3], hstat[:, 3:4],
                              ALU.subtract, ALU.mult)
        V(p1_vec2)

        t_p1_0 = sT.n
        s_p1e_0 = sS.n
        def p1_tensor(eng):
            for lt in range(NLT):
                eng.wait_ge(sV.h, v_xn1_0 + lt + 1)
                for half in range(2):
                    idx = lt * 2 + half
                    eng.fence()
                    if idx >= 2:
                        eng.wait_ge(sS.h, s_p1e_0 + idx - 1)
                    pv = pbig_b[:, half * 1024:(half + 1) * 1024]
                    for kk in range(8):
                        k = half * 8 + kk
                        eng.transpose(
                            pv[:, kk * 128:(kk + 1) * 128],
                            xn1T[:, lt * HID + k * 128:lt * HID + (k + 1) * 128],
                            identb[:])
            eng.wait_ge(sV.h, v_xn1_0 + NLT + 1)
            eng.wait_ge(sS.h, s_p1e_0 + 15)
            for k in range(16):
                eng.transpose(pE_b[:, k * 4:k * 4 + 3],
                              xn1h[:, k * 128:(k + 1) * 128], identb[0:3, 0:3])
        TE(p1_tensor)

        def p1_scalar3(eng):
            for lt in range(NLT):
                for half in range(2):
                    idx = lt * 2 + half
                    eng.wait_ge(sT.h, t_p1_0 + 8 * (idx + 1))
                    k0 = half * 8
                    eng.copy(
                        xn1F[:].rearrange("p (k t) -> p k t", k=16)
                        [:, k0:k0 + 8, lt * 128:(lt + 1) * 128],
                        pbig_b[:, half * 1024:(half + 1) * 1024]
                        .rearrange("p (k t) -> p k t", k=8))
            eng.wait_ge(sT.h, t_p1_0 + 128 + 16)
            eng.copy(
                xn1hF[:].rearrange("p (k t) -> p k t", k=16),
                pE_b[:, 0:64].rearrange("p (k t) -> p k t", k=16)[:, :, 0:3])
        S(p1_scalar3)

        barrier()
        dbg_store("xn1F", xn1F[:, :], [128, 16 * T], BF16)
        dbg_store("xn1hF", xn1hF[:], [128, 48], BF16)
        barrier()

        hbc_buf = [vf(OFF_B, 1027), vf(OFF_B + 1027, 1027)]
        cacc = [vf(OFF_B + 2054, 1024), vf(OFF_B + 3078, 1024)]
        hbs = [vb(OFF_B + 4102, 512), vb(OFF_B + 4614, 512)]
        wslab = [vb(OFF_B + 5126, 1024), vb(OFF_B + 6150, 1024)]

        T0 = sT.n; S0 = sS.n; V0 = sV.n
        CA0, CB0 = sCastA.n, sCastB.n
        mainbank = [(pA0, pA1), (pB0, pB1)]
        halobank = [pD, pE]

        def p2_castdma(eng):
            for d in range(NHB):
                if d >= 2:
                    eng.wait_ge(sT.h, T0 + 11 * (d - 2) + 3)
                sem = (sCastA, sCastB)[d % 2]
                sem.dma(eng.dma_start(
                    out=wslab[d % 2][:],
                    in_=I["w_hbc"][d].rearrange("i k j -> i (k j)")))
                if d % 2 == 0 and d // 2 <= 11:
                    i = d // 2
                    if i >= 2:
                        eng.wait_ge(sV.h, V0 + 5 * (2 * i - 2) + 5)
                    (sWs, sWs2)[i % 2].dma(eng.dma_start(
                        out=wsl[i % 2][:],
                        in_=I[("ws1", "ws2", "ws3")[i // 4]]
                        [:, (i % 4) * 2816:(i % 4 + 1) * 2816]))
        G(p2_castdma)

        def p2_tensor(eng):
            for d in range(NHB):
                eng.wait_ge((sCastA, sCastB)[d % 2].h,
                            (CA0, CB0)[d % 2] + (d // 2 + 1) * 16)
                if d >= 2:
                    eng.wait_ge(sS.h, S0 + 5 * (d - 2) + 3)
                mb = mainbank[d % 2]
                hb = halobank[d % 2]
                for k in range(16):
                    lhs = wslab[d % 2][:, k * 128:(k + 1) * 128]
                    eng.matmul(mb[0][:], lhs, xn1F[:, k * T:k * T + 512],
                               start=(k == 0), stop=(k == 15))
                    eng.matmul(mb[1][:], lhs, xn1F[:, k * T + 512:(k + 1) * T],
                               start=(k == 0), stop=(k == 15))
                    eng.matmul(hb[:, 0:3], lhs, xn1hF[:, k * 3:(k + 1) * 3],
                               start=(k == 0), stop=(k == 15))
                dd = d - 1
                eng.fence()
                if 0 <= dd <= 32:
                    eng.wait_ge(sS.h, S0 + 5 * dd + 4)
                    if dd >= 2:
                        eng.wait_ge(sS.h, S0 + 5 * (dd - 2) + 5)
                    pv = pbig_b[:, (dd % 2) * 1024:((dd % 2) + 1) * 1024]
                    src = hbs[dd % 2] if dd < 32 else B_F
                    for lt in range(8):
                        eng.transpose(pv[:, lt * 128:(lt + 1) * 128],
                                      src[:, lt * 128:(lt + 1) * 128], identb[:])
                else:
                    eng.pad(8)
        TE(p2_tensor)

        def p2_scalar(eng):
            for d in range(NHB):
                eng.wait_ge(sT.h, T0 + 11 * d + 3)
                if d >= 2 and d - 2 < 34:
                    eng.wait_ge(sV.h, V0 + 5 * (d - 1))
                mb = mainbank[d % 2]
                bias = b_cols[:, 32 + d:33 + d]
                if d < 34:
                    eng.activation(hbc_buf[d % 2][:, 3:515], mb[0][:], AF.Identity,
                                   bias=bias)
                    eng.activation(hbc_buf[d % 2][:, 515:1027], mb[1][:],
                                   AF.Identity, bias=bias)
                    eng.activation(hbc_buf[d % 2][:, 0:3], halobank[d % 2][:, 0:3],
                                   AF.Identity, bias=mb_cols[:, d:d + 1],
                                   scale=halo_mask[:, 0:1])
                else:
                    eng.activation(dtraw[:, 0:512], mb[0][0:64, :], AF.Identity,
                                   bias=bias[0:64, :])
                    eng.activation(dtraw[:, 512:1024], mb[1][0:64, :], AF.Identity,
                                   bias=bias[0:64, :])
                    eng.pad(1)
                if d < 34:
                    eng.wait_ge(sV.h, V0 + 5 * d + 4)
                    dst = (hbs[d % 2][:] if d < 32 else
                           (B_F[:] if d == 32 else C_F[:]))
                    eng.activation(dst, cacc[d % 2][:], AF.Silu)
                else:
                    eng.pad(1)
                if d <= 32:
                    eng.wait_ge(sT.h, T0 + 11 * (d + 1) + 11)
                    pv = pbig_b[:, (d % 2) * 1024:((d % 2) + 1) * 1024]
                    if d < 32:
                        eng.copy(
                            hsT[:, :].rearrange("p (l f) -> p l f", l=NLT)
                            [:, :, d * 128:(d + 1) * 128],
                            pv.rearrange("p (l f) -> p l f", l=8))
                    else:
                        eng.copy(
                            B_Tt[:].rearrange("p (l f) -> p l f", l=NCH),
                            pv.rearrange("p (l f) -> p l f", l=8))
                else:
                    eng.pad(1)
        S(p2_scalar)

        def p2_conv(eng):
            for d in range(34):
                eng.wait_ge(sS.h, S0 + 5 * d + 3)
                if d >= 2:
                    eng.wait_ge(sS.h, S0 + 5 * (d - 2) + 4)
                buf = hbc_buf[d % 2]; ca = cacc[d % 2]
                eng.tensor_scalar(ca[:], buf[:, 0:1024],
                                  convw[:, d * 4:d * 4 + 1], None, ALU.mult)
                for k in (1, 2, 3):
                    eng.scalar_tensor_tensor(ca[:], buf[:, k:k + 1024],
                                             convw[:, d * 4 + k:d * 4 + k + 1],
                                             ca[:], ALU.mult, ALU.add)
                if d % 2 == 0 and 2 <= d <= 24:
                    i = d // 2 - 1
                    eng.wait_ge((sWs, sWs2)[i % 2].h, 16 * (i // 2 + 1))
                    eng.tensor_reduce(wcol[:, i:i + 1], wsl[i % 2][:], AX,
                                      ALU.add, apply_absolute_value=True)
                else:
                    eng.pad(1)
            for i in range(3):
                eng.tensor_add(wpart[:, i:i + 1], wcol[:, 4 * i:4 * i + 1],
                               wcol[:, 4 * i + 1:4 * i + 2])
                eng.tensor_add(wpart[:, i:i + 1], wpart[:, i:i + 1],
                               wcol[:, 4 * i + 2:4 * i + 3])
                eng.tensor_add(wpart[:, i:i + 1], wpart[:, i:i + 1],
                               wcol[:, 4 * i + 3:4 * i + 4])
        V(p2_conv)

        barrier()
        dbg_store("hsT", hsT[:, 0:NLT * INTER], [128, NLT * INTER], BF16)
        dbg_store("B_F", B_F[:], [128, T], BF16)
        dbg_store("C_F", C_F[:], [128, T], BF16)
        dbg_store("B_Tt", B_Tt[:], [128, NCH * 128], BF16)
        dbg_store("dtraw", dtraw[:], [64, T])
        barrier()

        gslab = [vb(OFF_D, 2048), vb(OFF_D + 2048, 2048), vb(OFF_D + 4096, 2048)]
        bgr = vf(OFF_D + 6144, 512, rows=1)

        T0b = sT.n; S0b = sS.n; L0b = sLdB.n
        CR0 = [sCastA.n, sCastB.n, sCastC.n]

        def p2b_ld(eng):
            for n in range(8):
                if n >= 1:
                    eng.wait_ge(sT.h, T0b + 8 * n)
                sLdB.dma(eng.dma_start(
                    out=bgr,
                    in_=I["b_gate_row"][0:1, n * 512:(n + 1) * 512]))
        SY(p2b_ld)

        def p2b_castdma(eng):
            for n in range(8):
                for q in range(2):
                    si = 2 * n + q
                    if si >= 3:
                        n_old = (si - 3) // 2
                        eng.wait_ge(sT.h, T0b + 8 * (n_old + 1))
                    sem = (sCastA, sCastB, sCastC)[si % 3]
                    sem.dma(eng.dma_start(
                        out=gslab[si % 3][:],
                        in_=I["w_gate"][n].rearrange("i k j -> i (k j)")
                        [:, q * 8 * 512:(q + 1) * 8 * 512]))
        G(p2b_castdma)

        def p2b_tensor(eng):
            it = 0
            for n in range(8):
                eng.wait_ge(sLdB.h, L0b + (n + 1) * 16)
                for si in (2 * n, 2 * n + 1):
                    eng.wait_ge((sCastA, sCastB, sCastC)[si % 3].h,
                                CR0[si % 3] + (si // 3 + 1) * 16)
                g0 = gslab[(2 * n) % 3]
                g1 = gslab[(2 * n + 1) % 3]
                for t in range(8):
                    bank = (pA0, pA1, pB0, pB1)[it % 4]
                    eng.fence()
                    if it >= 4:
                        eng.wait_ge(sS.h, S0b + it - 3)
                    for k in range(16):
                        gs = g0 if k < 8 else g1
                        eng.matmul(bank[:],
                                   xn1F[:, k * T + t * 128:k * T + (t + 1) * 128],
                                   gs[:, (k % 8) * 512:(k % 8 + 1) * 512],
                                   start=(k == 0), stop=False)
                    eng.matmul(bank[:], ones_row[:], bgr,
                               start=False, stop=True)
                    it += 1
        TE(p2b_tensor)

        def p2b_scalar(eng):
            it = 0
            for n in range(8):
                for t in range(8):
                    bank = (pA0, pA1, pB0, pB1)[it % 4]
                    eng.wait_ge(sT.h, T0b + it + 1)
                    eng.activation(
                        sgT[:, t * INTER + n * 512:t * INTER + (n + 1) * 512],
                        bank[:], AF.Silu)
                    it += 1
        S(p2b_scalar)

        barrier()
        dbg_store("sgT", sgT[:, 0:NLT * INTER], [128, NLT * INTER], BF16)
        barrier()

        S_in = vb(0, 2048)
        dtT = vb(4096, 256)
        dtdT = vb(4352, 256)
        sdT = vb(4608, 256)
        dkE = vf(4864, 512)
        acum = vf(5376, 1024, rows=64)
        dt_sp = vf(6400, 1024, rows=64)
        abt_scr = vf(7424, 128, rows=64)
        acumT = vf(7552, 512)
        dmj_scr = vf(8064, 64)
        omask = vf(8128, 7)
        wsum = vf(8135, 3)
        pE_scr3 = vf(8138, 1, rows=3)
        OFF_E = 47616
        AFLAT = vf(OFF_E, 1024, rows=1)
        sp4 = vf(OFF_E, 1024, rows=64)
        sdF_scr = vf(OFF_D, 128, rows=64)
        dtd_scr = vf(OFF_D + 128, 128, rows=64)
        M_grp = vb(6400, 512)
        Gm = vb(6912, 64)
        scrY7 = vf(6976, 512)
        hsdd = vb(OFF_D, 2048)
        hs_dt = vb(OFF_D + 2048, 2048)
        xg = vb(OFF_D + 4096, 2048)
        hsD_scr = vb(OFF_D + 6144, 256)
        hsDr = [hsD_scr, vb(OFF_D + 6400, 256)]
        scrY = vf(OFF_D + 2048, 512)

        def p4_scalar(eng):
            eng.activation(sp4[:], dtraw[:], AF.Abs, bias=dtbias2[:])
            eng.activation(sp4[:], sp4[:], AF.Exp, scale=-1.0)
            eng.activation(sp4[:], sp4[:], AF.Ln, bias=ones_col[0:64, :])
        S(p4_scalar)
        c4s = sS.n

        def p4_vec(eng):
            eng.tensor_scalar(dt_sp[:], dtraw[:], dtbias2[:], 0.0, ALU.add, ALU.max)
            eng.wait_ge(sS.h, c4s)
            eng.tensor_add(dt_sp[:], dt_sp[:], sp4[:])
            for ch in range(NCH):
                eng.tensor_scalar(abt_scr[:], dt_sp[:, ch * 128:(ch + 1) * 128],
                                  A_col[:], None, ALU.mult)
                eng.tensor_tensor_scan(acum[:, ch * 128:(ch + 1) * 128],
                                       abt_scr[:], abt_scr[:], 0.0,
                                       ALU.add, ALU.bypass)
            eng.memset(akrow[0:1, 576:640], 0.0)
        V(p4_vec)
        c4v = sV.n

        t4_0 = sT.n
        s4_0 = sS.n
        v4_0 = sV.n

        def p4b_vec(eng):
            for ch in range(NCH):
                eng.wait_ge(sS.h, s4_0 + 8 * ch + 1)
                eng.tensor_mul(dtd_scr[:], dt_sp[:, ch * 128:(ch + 1) * 128],
                               dtd_scr[:])
        V(p4b_vec)

        def p4b_tensor(eng):
            for ch in range(NCH):
                eng.fence()
                eng.wait_ge(sV.h, v4_0 + ch + 1)
                eng.wait_ge(sS.h, s4_0 + 8 * ch + 2)
                if ch >= 1:
                    eng.wait_ge(sS.h, s4_0 + 8 * (ch - 1) + 7)
                eng.transpose(pE[:, 0:64], dt_sp[:, ch * 128:(ch + 1) * 128],
                              identf[0:64, 0:64])
                eng.transpose(pE[:, 64:128], dtd_scr[:], identf[0:64, 0:64])
                eng.transpose(pE[:, 128:192], sdF_scr[:], identf[0:64, 0:64])
                eng.transpose(pE[0:1, 192:256],
                              acum[:, ch * 128 + 127:ch * 128 + 128],
                              identf[0:64, 0:64])
                eng.transpose(pD[:, 64:128], acum[:, ch * 128:(ch + 1) * 128],
                              identf[0:64, 0:64])
                eng.fence()
                eng.wait_ge(sS.h, s4_0 + 8 * ch + 7)
                if ch >= 1:
                    eng.wait_ge(sS.h, s4_0 + 8 * (ch - 1) + 8)
                eng.matmul(pD[:, 0:64], ones_row[:],
                           akrow[0:1, ch * 64:(ch + 1) * 64], start=True, stop=True)
        TE(p4b_tensor)

        def p4b_scalar(eng):
            for ch in range(NCH):
                akcol = acum[:, ch * 128 + 127:ch * 128 + 128]
                eng.wait_ge(sV.h, c4v)
                eng.activation(dtd_scr[:], acum[:, ch * 128:(ch + 1) * 128],
                               AF.Exp, bias=akcol, scale=-1.0)
                eng.activation(sdF_scr[:], acum[:, ch * 128:(ch + 1) * 128], AF.Exp)
                eng.wait_ge(sT.h, t4_0 + 6 * ch + 5)
                eng.copy(dtT[:, ch * 64:(ch + 1) * 64], pE[:, 0:64])
                eng.copy(dtdT[:, ch * 64:(ch + 1) * 64], pE[:, 64:128])
                eng.copy(sdT[:, ch * 64:(ch + 1) * 64], pE[:, 128:192])
                eng.copy(akrow[0:1, ch * 64:(ch + 1) * 64], pE[0:1, 192:256])
                eng.copy(acumT[:, ch * 64:(ch + 1) * 64], pD[:, 64:128])
                eng.wait_ge(sT.h, t4_0 + 6 * ch + 6)
                eng.activation(dkE[:, ch * 64:(ch + 1) * 64], pD[:, 0:64], AF.Exp)
        S(p4b_scalar)
        c4b_s = sS.n

        def p4c_vec(eng):
            eng.wait_ge(sS.h, c4b_s)
            eng.tensor_reduce(
                akrow[0:1, 512:576],
                akrow[0:1, 0:512].rearrange("p (c h) -> p h c", c=NCH), AX, ALU.add)
        V(p4c_vec)
        c4c_v = sV.n

        def p4c_tensor(eng):
            eng.fence()
            eng.wait_ge(sV.h, c4c_v)
            eng.wait_ge(sS.h, c4b_s)
            eng.matmul(pD[:, 64:128], ones_row[:], akrow[0:1, 512:576],
                       start=True, stop=True)
            eng.matmul(pD[:, 128:192], ones_row[:], d_rowt[:], start=True, stop=True)
        TE(p4c_tensor)
        c4c_t = sT.n

        def p4c_scalar(eng):
            eng.wait_ge(sT.h, c4c_t)
            eng.activation(dtil[:], pD[:, 64:128], AF.Exp)
            eng.copy(Dtile[:], pD[:, 128:192])
        S(p4c_scalar)

        barrier()
        dbg_store("acum", acum[:], [64, T])
        dbg_store("dt_sp", dt_sp[:], [64, T])
        dbg_store("dtT", dtT[:, :], [128, 512], BF16)
        dbg_store("dtdT", dtdT[:, :], [128, 512], BF16)
        dbg_store("sdT", sdT[:, :], [128, 512], BF16)
        dbg_store("dkE", dkE[:], [128, 512])
        dbg_store("dtil", dtil[:], [128, 64])
        barrier()

        t5_0 = sT.n
        v5_0 = sV.n

        def p5_vec(eng):
            eng.memset(S_in[:], 0.0)
            for ch in range(NCH):
                if ch >= 1:
                    eng.wait_ge(sT.h, t5_0 + 8 * ch)
                eng.tensor_tensor(
                    hsdd[:].rearrange("p (h q) -> p h q", h=64),
                    hsT[:, ch * 4096:(ch + 1) * 4096]
                    .rearrange("p (h q) -> p h q", h=64),
                    dtdT[:, ch * 64:(ch + 1) * 64][:, :, None]
                    .broadcast_to([128, 64, 64]),
                    ALU.mult)
                for g in range(8):
                    eng.wait_ge(sT.h, t5_0 + 8 * ch + g + 1)
                    eng.tensor_tensor(
                        scrY[:].rearrange("p (h q) -> p h q", h=8),
                        S_in[:, g * 512:(g + 1) * 512]
                        .rearrange("p (h q) -> p h q", h=8),
                        dkE[:, ch * 64 + g * 8:ch * 64 + (g + 1) * 8][:, :, None]
                        .broadcast_to([128, 8, 64]),
                        ALU.mult)
                    eng.tensor_add(S_in[:, g * 512:(g + 1) * 512], scrY[:],
                                   (pD if g % 2 == 0 else pE)[:])
        def p5_tensor(eng):
            for ch in range(NCH):
                eng.fence()
                eng.wait_ge(sV.h, v5_0 + 1 + ch * 17 + 1)
                for g in range(8):
                    bank = pD if g % 2 == 0 else pE
                    if g >= 2:
                        eng.wait_ge(sV.h, v5_0 + 1 + ch * 17 + 1 + 2 * (g - 1))
                    eng.matmul(bank[:],
                               B_Tt[:, ch * 128:(ch + 1) * 128],
                               hsdd[:, g * 512:(g + 1) * 512],
                               start=True, stop=True)
                    eng.fence()
        TE(p5_tensor)
        V(p5_vec)

        barrier()
        dbg_store("O_st", S_in[:], [128, 4096], BF16)
        barrier()

        def p5c_sync(eng):
            sLd.dma(eng.dma_start(out=cc_in[:, 4096:4160], in_=dtil[:]))
            sLd.dma(eng.dma_start(out=cc_in[:, 4160:4164], in_=wpart[:]))
        SY(p5c_sync)
        c5_ld = sLd.n

        def p5c_gp(eng):
            sCastC.dma(eng.dma_start(out=cc_in[:, 0:4096], in_=S_in[:]))
            eng.wait_ge(sCastC.h, sCastC.n)
            eng.wait_ge(sLd.h, c5_ld)
            eng.collective_compute(
                "AllGather", ALU.bypass, [list(range(NCORE))],
                [cc_in[:]], [cc_out[:]]).then_inc(sCC.h, 1)
        block.gpsimd(p5c_gp)
        sCC.n += 1
        c6_castc = sCastC.n

        rb = vf(OFF_D, 4164)
        qthr = sb("qthr", [128, 10])

        v6_rd = {}
        LA0, LB0 = sLdA.n, sLdB.n

        def p6_vec(eng):
            eng.tensor_scalar(omask[:], smask[:], -1.0, 1.0, ALU.mult, ALU.add)
            eng.memset(wsum[:], 0.0)
            eng.wait_ge(sCastC.h, c6_castc)
            eng.memset(S_in[:], 0.0)
            for j in range(NCORE):
                eng.wait_ge((sLdA, sLdB)[j % 2].h,
                            (LA0, LB0)[j % 2] + (j // 2 + 1) * 16)
                eng.tensor_add(wsum[:], wsum[:], rb[:, 4160:4163])
                if j < 7:
                    eng.scalar_tensor_tensor(
                        dmj_scr[:], rb[:, 4096:4160], smask[:, j:j + 1],
                        omask[:, j:j + 1].broadcast_to([128, 64]),
                        ALU.mult, ALU.add)
                    eng.tensor_tensor(
                        S_in[:].rearrange("p (h q) -> p h q", h=64),
                        S_in[:].rearrange("p (h q) -> p h q", h=64),
                        dmj_scr[:][:, :, None].broadcast_to([128, 64, 64]),
                        ALU.mult)
                    eng.scalar_tensor_tensor(
                        S_in[:], rb[:, 0:4096], smask[:, j:j + 1], S_in[:],
                        ALU.mult, ALU.add)
                v6_rd[j] = sV.n
        V(p6_vec)

        def p6_sync(eng):
            eng.wait_ge(sCC.h, sCC.n)
            for j in range(NCORE):
                if j >= 1:
                    eng.wait_ge(sV.h, v6_rd[j - 1])
                sem = (sLdA, sLdB)[j % 2]
                sem.dma(eng.dma_start(out=rb[:],
                                      in_=cc_out[j * 128:(j + 1) * 128, :]))
        SY(p6_sync)
        c6_v = sV.n

        def p6_tensor(eng):
            eng.fence()
            eng.wait_ge(sV.h, c6_v)
            eng.matmul(pE[0:3, 0:1], wsum[:, 0:3], ones_col[:], start=True, stop=True)
        TE(p6_tensor)
        c6_t = sT.n

        def p6_scalar_a(eng):
            eng.wait_ge(sT.h, c6_t)
            eng.copy(pE_scr3[:], pE[0:3, 0:1])
        S(p6_scalar_a)
        c6_s = sS.n

        def p6_tensor_b(eng):
            eng.fence()
            eng.wait_ge(sS.h, c6_s)
            eng.transpose(pD[0:1, 0:3], pE_scr3[:], identf[0:3, 0:3])
        TE(p6_tensor_b)
        c6_t2 = sT.n

        def p6_scalar_b(eng):
            eng.wait_ge(sT.h, c6_t2)
            eng.copy(akrow[0:1, 576:579], pD[0:1, 0:3])
        S(p6_scalar_b)
        c6_s2 = sS.n

        def p6_tensor_c(eng):
            eng.fence()
            eng.wait_ge(sS.h, c6_s2)
            eng.matmul(pD[:, 192:256], ones_row[:],
                       akrow[0:1, 576:640], start=True, stop=True)
        TE(p6_tensor_c)
        c6_t3 = sT.n

        def p6_vec_b(eng):
            eng.wait_ge(sT.h, c6_t3)
            eng.tensor_scalar(wk[:, 0:3], pD[:, 192:195], 1.0 / WCNT, None, ALU.mult)
            eng.tensor_scalar(qthr[:, 0:3], wk[:, 0:3], 0.5, None, ALU.mult)
            eng.tensor_scalar(qthr[:, 3:6], qthr[:, 0:3], -1.0, None, ALU.mult)
            eng.tensor_scalar(qthr[:, 6:7], wk[:, 0:1], 0.5 / 127.0, None, ALU.mult)
            eng.tensor_scalar(qthr[:, 7:8], wk[:, 2:3], 0.5 / 127.0, None, ALU.mult)
            eng.tensor_scalar(qthr[:, 8:9], wk[:, 1:2], 0.5 / 127.0, None, ALU.mult)
        V(p6_vec_b)

        barrier()
        dbg_store("S0", S_in[:], [128, 4096], BF16)
        dbg_store("wk", wk[:, 0:3], [128, 3])
        barrier()

        rms_ms = sb("rms_ms", [128, NCH]); rms_rs = sb("rms_rs", [128, NCH])
        t7_0 = sT.n; v7_0 = sV.n; s7_0 = sS.n
        L7A = sLdA.n

        def te_L_pos(ch, g):
            return t7_0 + 41 * ch + 9 + (2 if g == 0 else 4 * g)

        def te_Yd_pos(ch, g):
            return t7_0 + 41 * ch + 9 + (4 * g + 5 if g < 7 else 31)

        def te_Yoff_pos(ch, g):
            return t7_0 + 41 * ch + 9 + (4 * g + 6 if g < 7 else 32)

        def v_pos(ch, k):
            return v7_0 + 63 * ch + k

        def s_pos(ch, k):
            return s7_0 + 19 * ch + k

        g7_0 = sG.n

        def g_pos(ch, k):
            return g7_0 + 10 * ch + k

        def p7_sync(eng):
            for ch in range(NCH):
                for g in range(8):
                    idx = ch * 8 + g
                    if idx >= 1:
                        pch, pg = divmod(idx - 1, 8)
                        eng.wait_ge(sT.h, te_L_pos(pch, pg))
                    sLdA.dma(eng.dma_start(
                        out=AFLAT,
                        in_=acum[g * 8:(g + 1) * 8, ch * 128:(ch + 1) * 128]))
        SY(p7_sync)

        def p7_gp(eng):
            for ch in range(NCH):
                if ch >= 1:
                    eng.wait_ge(sT.h, t7_0 + 41 * ch)
                    eng.tensor_tensor(
                        hsdd[:].rearrange("p (h q) -> p h q", h=64),
                        hsT[:, (ch - 1) * 4096:ch * 4096]
                        .rearrange("p (h q) -> p h q", h=64),
                        dtdT[:, (ch - 1) * 64:ch * 64][:, :, None]
                        .broadcast_to([128, 64, 64]), ALU.mult)
                else:
                    eng.pad(1)
                if ch >= 1:
                    eng.wait_ge(sS.h, s_pos(ch - 1, 19))
                eng.tensor_tensor(
                    hs_dt[:].rearrange("p (h q) -> p h q", h=64),
                    hsT[:, ch * 4096:(ch + 1) * 4096]
                    .rearrange("p (h q) -> p h q", h=64),
                    dtT[:, ch * 64:(ch + 1) * 64][:, :, None]
                    .broadcast_to([128, 64, 64]), ALU.mult)
                for g in range(8):
                    if g >= 2:
                        eng.wait_ge(sT.h, te_Yd_pos(ch, g - 2))
                    elif ch >= 1:
                        eng.wait_ge(sT.h, te_Yd_pos(ch - 1, 6 + g))
                    eng.tensor_tensor(
                        hsDr[g % 2][:].rearrange("p (h q) -> p h q", h=8),
                        hsT[:, ch * 4096 + g * 512:ch * 4096 + (g + 1) * 512]
                        .rearrange("p (h q) -> p h q", h=8),
                        Dtile[:, g * 8:(g + 1) * 8][:, :, None]
                        .broadcast_to([128, 8, 64]), ALU.mult)
        G(p7_gp)

        def p7_tensor(eng):
            for ch in range(NCH):
                if ch >= 1:
                    for g in range(8):
                        eng.fence()
                        eng.wait_ge(sG.h, g_pos(ch, 1))
                        if g >= 2:
                            eng.wait_ge(sV.h, v_pos(ch, 1 + 2 * (g - 1)))
                        bank = pD if g % 2 == 0 else pE
                        eng.matmul(bank[:],
                                   B_Tt[:, (ch - 1) * 128:ch * 128],
                                   hsdd[:, g * 512:(g + 1) * 512],
                                   start=True, stop=True)
                else:
                    eng.pad(8)
                eng.fence()
                if ch >= 1:
                    eng.wait_ge(sV.h, v_pos(ch - 1, 59))
                eng.matmul(pbig[:, 0:128], B_F[:, ch * 128:(ch + 1) * 128],
                           C_F[:, ch * 128:(ch + 1) * 128], start=True, stop=True)
                def ysec(j, need_m=False):
                    eng.fence()
                    if need_m:
                        eng.wait_ge(sV.h, v_pos(ch, 19 + 5 * j + 5))
                    if j == 0:
                        eng.wait_ge(sG.h, g_pos(ch, 2))
                    ydb = pA0 if j % 2 == 0 else pB0
                    yob = pA1 if j % 2 == 0 else pB1
                    if j >= 2:
                        eng.wait_ge(sV.h, v_pos(ch, 19 + 5 * (j - 1) + 3))
                    elif ch >= 1:
                        eng.wait_ge(sV.h, v_pos(ch - 1, (57, 61)[j]))
                    for jj in range(8):
                        h = j * 8 + jj
                        eng.matmul(ydb[:, jj * 64:(jj + 1) * 64],
                                   M_grp[:, jj * 128:(jj + 1) * 128],
                                   hs_dt[:, h * 64:(h + 1) * 64],
                                   start=(jj == 0), stop=False)
                    eng.wait_ge(sG.h, g_pos(ch, 3 + j))
                    eng.matmul(ydb[:], identb[:], hsDr[j % 2][:],
                               start=False, stop=True)
                    eng.wait_ge(sV.h, v_pos(ch, 2 * j + 3))
                    eng.matmul(yob[:], C_F[:, ch * 128:(ch + 1) * 128],
                               S_in[:, j * 512:(j + 1) * 512],
                               start=True, stop=True)

                for g in range(8):
                    idx = ch * 8 + g
                    eng.fence()
                    eng.wait_ge(sLdA.h, L7A + (idx + 1) * 16)
                    if g >= 1:
                        eng.wait_ge(sV.h, v_pos(ch, 19 + 5 * (g - 1) + 5))
                    else:
                        if ch >= 1:
                            eng.wait_ge(sV.h, v_pos(ch - 1, 59))
                        eng.wait_ge(sV.h, v_pos(ch, 19))
                    for hh in range(2):
                        eng.matmul(pbig[:, hh * 512:(hh + 1) * 512], ones_row[:],
                                   AFLAT[0:1, hh * 512:(hh + 1) * 512],
                                   start=True, stop=True)
                    if g >= 1:
                        ysec(g - 1)
                ysec(7, need_m=True)
        TE(p7_tensor)

        def p7_vec(eng):
            for ch in range(NCH):
                eng.pad(1)
                if ch >= 1:
                    for g in range(8):
                        eng.wait_ge(sT.h, t7_0 + 41 * ch + g + 1)
                        bank = pD if g % 2 == 0 else pE
                        eng.tensor_tensor(
                            scrY7[:].rearrange("p (h q) -> p h q", h=8),
                            S_in[:, g * 512:(g + 1) * 512]
                            .rearrange("p (h q) -> p h q", h=8),
                            dkE[:, (ch - 1) * 64 + g * 8:(ch - 1) * 64 + (g + 1) * 8]
                            [:, :, None].broadcast_to([128, 8, 64]), ALU.mult)
                        eng.tensor_add(S_in[:, g * 512:(g + 1) * 512], scrY7[:],
                                       bank[:])
                else:
                    eng.pad(16)
                eng.pad(1)
                eng.wait_ge(sT.h, t7_0 + 41 * ch + 9)
                eng.tensor_tensor(Gm[:], pbig[:, 0:128], trimask[:], ALU.mult)

                def ychain(g):
                    ydb = pA0 if g % 2 == 0 else pB0
                    yob = pA1 if g % 2 == 0 else pB1
                    eng.wait_ge(sT.h, te_Yoff_pos(ch, g))
                    eng.tensor_tensor(
                        scrY7[:].rearrange("p (h q) -> p h q", h=8),
                        yob[:].rearrange("p (h q) -> p h q", h=8),
                        sdT[:, ch * 64 + g * 8:ch * 64 + (g + 1) * 8][:, :, None]
                        .broadcast_to([128, 8, 64]), ALU.mult)
                    eng.tensor_add(scrY7[:], scrY7[:], ydb[:])
                    eng.tensor_tensor(
                        xg[:, g * 512:(g + 1) * 512], scrY7[:],
                        sgT[:, ch * 4096 + g * 512:ch * 4096 + (g + 1) * 512],
                        ALU.mult)

                for g in range(8):
                    eng.wait_ge(sT.h, te_L_pos(ch, g))
                    eng.tensor_tensor(
                        pbig[:].rearrange("p (h q) -> p h q", h=8),
                        pbig[:].rearrange("p (h q) -> p h q", h=8),
                        acumT[:, ch * 64 + g * 8:ch * 64 + (g + 1) * 8][:, :, None]
                        .broadcast_to([128, 8, 128]), ALU.subtract)
                    if g >= 1:
                        ychain(g - 1)
                    else:
                        eng.pad(3)
                    eng.wait_ge(sS.h, s_pos(ch, 2 * g + 2))
                    eng.tensor_tensor(
                        M_grp[:].rearrange("p (h q) -> p h q", h=8),
                        pbig[:].rearrange("p (h q) -> p h q", h=8),
                        Gm[:][:, None, :].broadcast_to([128, 8, 128]), ALU.mult)
                ychain(7)
                eng.wait_ge(sS.h, s_pos(ch, 19))
                eng.tensor_scalar(sgT[:, ch * 4096:(ch + 1) * 4096], xg[:],
                                  rms_rs[:, ch:ch + 1], None, ALU.mult)
        V(p7_vec)

        def p7_scalar(eng):
            for ch in range(NCH):
                for g in range(8):
                    eng.wait_ge(sV.h, v_pos(ch, 19 + 5 * g + 1))
                    eng.activation(pbig[:], pbig[:], AF.Relu, scale=-1.0)
                    eng.activation(pbig[:], pbig[:], AF.Exp, scale=-1.0)
                eng.wait_ge(sV.h, v_pos(ch, 62))
                eng.wait_ge(sT.h, t7_0 + 41 * ch + 41)
                eng.activation(hs_dt[:], xg[:], AF.Square,
                               accum_out=rms_ms[:, ch:ch + 1])
                eng.activation(rms_rs[:, ch:ch + 1], rms_ms[:, ch:ch + 1],
                               AF.Ln, bias=eps_col[:], scale=1.0 / INTER)
                eng.activation(rms_rs[:, ch:ch + 1], rms_rs[:, ch:ch + 1],
                               AF.Exp, scale=-0.5)
        S(p7_scalar)

        barrier()
        dbg_store("xgn", sgT[:, 0:NLT * INTER], [128, NLT * INTER], BF16)
        barrier()

        t8_0 = sT.n; s8_0 = sS.n
        def p8_tensor(eng):
            for ch in range(NCH):
                for b in range(4):
                    idx = ch * 4 + b
                    eng.fence()
                    if idx >= 2:
                        eng.wait_ge(sS.h, s8_0 + idx - 1)
                    pv = pbig_b[:, (idx % 2) * 1024:((idx % 2) + 1) * 1024]
                    for jj in range(8):
                        j = b * 8 + jj
                        eng.transpose(
                            pv[:, jj * 128:(jj + 1) * 128],
                            sgT[:, ch * 4096 + j * 128:ch * 4096 + (j + 1) * 128],
                            identb[:])
        TE(p8_tensor)

        def p8_scalar(eng):
            for ch in range(NCH):
                for b in range(4):
                    idx = ch * 4 + b
                    eng.wait_ge(sT.h, t8_0 + 8 * (idx + 1))
                    eng.copy(
                        xgnF[:].rearrange("p (k t) -> p k t", k=32)
                        [:, b * 8:(b + 1) * 8, ch * 128:(ch + 1) * 128],
                        pbig_b[:, (idx % 2) * 1024:((idx % 2) + 1) * 1024]
                        .rearrange("p (k t) -> p k t", k=8))
        S(p8_scalar)

        barrier()

        wof = [vf(0, 2048), vf(2048, 2048)]
        wob = [vb(4096, 1024), vb(5120, 1024)]
        attn_scr = [vf(OFF_D, 1024), vf(OFF_D + 1024, 1024)]

        t9_0 = sT.n; v9_0 = sV.n; s9_0 = sS.n; g9_0 = sG.n
        L9A, L9B = sLdA.n, sLdB.n

        def p9_sync(eng):
            for lt in range(NLT):
                sLd.dma(eng.dma_start(out=x2[:, lt * HID:(lt + 1) * HID],
                                      in_=I["x"][lt * 128:(lt + 1) * 128, :]))
            for d in range(16):
                for hh in range(2):
                    idx = d * 2 + hh
                    if idx >= 2:
                        if hh == 0:
                            eng.wait_ge(sV.h, v9_0 + 2 * (d - 1) + 1)
                        else:
                            eng.wait_ge(sG.h, g9_0 + d)
                    sem = (sLdA, sLdB)[idx % 2]
                    sem.dma(eng.dma_start(
                        out=wof[idx % 2][:],
                        in_=I["w_out"][d].rearrange("i k j -> i (k j)")
                        [:, hh * 2048:(hh + 1) * 2048]))
        SY(p9_sync)
        c9_ld = sLd.n

        def p9_gp(eng):
            for d in range(16):
                idx = d * 2 + 1
                eng.wait_ge((sLdA, sLdB)[idx % 2].h,
                            (L9A, L9B)[idx % 2] + (idx // 2 + 1) * 16)
                if d >= 1:
                    eng.wait_ge(sT.h, t9_0 + 10 * (d - 1) + 2)
                eng.tensor_tensor(
                    wob[idx % 2][:].rearrange("p (k j) -> p k j", k=16),
                    wof[idx % 2][:].rearrange("p (k j) -> p k j", k=16),
                    gam[:, 16:32][:, :, None]
                    .broadcast_to([128, 16, 128]), ALU.mult)
        G(p9_gp)

        def p9_vec(eng):
            for d in range(16):
                idx = d * 2
                eng.wait_ge((sLdA, sLdB)[idx % 2].h,
                            (L9A, L9B)[idx % 2] + (idx // 2 + 1) * 16)
                if d >= 1:
                    eng.wait_ge(sT.h, t9_0 + 10 * (d - 1) + 1)
                eng.tensor_tensor(
                    wob[idx % 2][:].rearrange("p (k j) -> p k j", k=16),
                    wof[idx % 2][:].rearrange("p (k j) -> p k j", k=16),
                    gam[:, 0:16][:, :, None]
                    .broadcast_to([128, 16, 128]), ALU.mult)
                eng.wait_ge(sT.h, t9_0 + 10 * d + 10)
                eng.wait_ge(sLd.h, c9_ld)
                eng.tensor_add(
                    x2[:].rearrange("p (l f) -> p l f", l=NLT)
                    [:, :, d * 128:(d + 1) * 128],
                    x2[:].rearrange("p (l f) -> p l f", l=NLT)
                    [:, :, d * 128:(d + 1) * 128],
                    pbig[:].rearrange("p (l f) -> p l f", l=8))
        V(p9_vec)

        def p9_tensor_real(eng):
            nt = 0
            for d in range(16):
                bankT = (pA0, pA1) if d % 2 == 0 else (pB0, pB1)
                for hh in range(2):
                    idx = d * 2 + hh
                    eng.fence()
                    if hh == 0:
                        eng.wait_ge(sV.h, v9_0 + 2 * d + 1)
                    else:
                        eng.wait_ge(sG.h, g9_0 + d + 1)
                    if d >= 2:
                        eng.wait_ge(sS.h, s9_0 + 2 * (d - 2) + 2)
                    for kk in range(16):
                        k = hh * 16 + kk
                        for th in range(2):
                            eng.matmul(bankT[th][:],
                                       wob[idx % 2][:, kk * 128:(kk + 1) * 128],
                                       xgnF[:, k * 1024 + th * 512:
                                            k * 1024 + (th + 1) * 512],
                                       start=(k == 0), stop=(k == 31))
                eng.fence()
                eng.wait_ge(sS.h, s9_0 + 2 * d + 2)
                if d >= 2:
                    eng.wait_ge(sV.h, v9_0 + 2 * (d - 2) + 2)
                for lt in range(8):
                    eng.transpose(pbig[:, lt * 128:(lt + 1) * 128],
                                  attn_scr[d % 2][:, lt * 128:(lt + 1) * 128],
                                  identf[:])
        TE(p9_tensor_real)

        def p9_scalar(eng):
            for d in range(16):
                bankT = (pA0, pA1) if d % 2 == 0 else (pB0, pB1)
                eng.wait_ge(sT.h, t9_0 + 10 * d + 2)
                for th in range(2):
                    eng.activation(attn_scr[d % 2][:, th * 512:(th + 1) * 512],
                                   bankT[th][:], AF.Identity,
                                   bias=b_out_cols[:, d:d + 1])
        S(p9_scalar)

        barrier()
        dbg_store("x2", x2[:, 0:NLT * HID], [128, NLT * HID])
        barrier()

        fq1 = sb("fq1", [128, 8]); fq3 = sb("fq3", [128, 8])
        s10 = vf(OFF_D, 2048)
        xqT = vb(OFF_D + 2048, 4096)
        scrF = vf(OFF_D, 512)
        scrF2 = vf(OFF_D + 512, 512)
        qv_scr = vf(OFF_D + 1024, 2048)
        xqF = vb(OFF_A, 4096)
        hbuf = vb(OFF_A + 4096, 11264)
        amax_acc = vf(OFF_A + 15360, 512)
        amin_acc = vf(OFF_A + 15872, 512)
        SCLH = vf(0, 512)
        RS1 = vf(0, 512); RS2 = vf(512, 512)
        rowbuf = vf(1024, 512, rows=1)
        t1bR = [vb(1536, 1024), vb(2560, 1024)]
        t3bR = [vb(3584, 1024), vb(4608, 1024)]
        sg4 = [vb(5632 + i * 512, 512) for i in range(4)]
        wring4 = [vf(OFF_D + 1024 + i * 1024, 1024) for i in range(4)]
        w2tR = [vb(1536, 2816), vb(4352, 2816)]
        w2r2 = [vf(OFF_D + 1024 + j * 2048, 2048) for j in range(2)]
        sgnw = [vb(OFF_D + 5120 + i * 1024, 1024) for i in range(2)]
        ffn_scr = vf(OFF_D, 512)

        def build_bcast(cols, dst):
            vpre = sV.n; spre = sS.n
            def _t1(eng, cols=cols, vpre=vpre, spre=spre):
                eng.fence()
                eng.wait_ge(sV.h, vpre)
                eng.wait_ge(sS.h, spre)
                for i, cap in enumerate(cols):
                    eng.transpose(pE[0:1, i * 128:(i + 1) * 128], cap, identf[:])
                eng.fence()
            TE(_t1)
            tpre = sT.n
            def _s1(eng, tpre=tpre):
                eng.wait_ge(sT.h, tpre)
                eng.copy(rowbuf, pE[0:1, 0:512])
            S(_s1)
            spost = sS.n; vpre2 = sV.n
            def _t2(eng, spost=spost, vpre2=vpre2):
                eng.fence()
                eng.wait_ge(sS.h, spost)
                eng.wait_ge(sV.h, vpre2)
                eng.matmul(pD[:], ones_row[:], rowbuf, start=True, stop=True)
                eng.fence()
            TE(_t2)
            tpost = sT.n
            def _v1(eng, dst=dst, tpost=tpost):
                eng.wait_ge(sT.h, tpost)
                eng.tensor_copy(dst, pD[:])
            V(_v1)

        for HF in range(2):
            lts = list(range(HF * 4, HF * 4 + 4))
            v10_0 = sV.n; s10_0 = sS.n

            stq = sSt.n

            def lnexp_cnt(i, s0=s10_0):
                return s0 + 4 * (i + 1) + 2 if i < 3 else s0 + 18

            def p10_scalar(eng, lts=lts, v10_0=v10_0, stq=stq):
                eng.wait_ge(sSt.h, stq)
                for i, lt in enumerate(lts):
                    x2t = x2[:, lt * HID:(lt + 1) * HID]
                    if i >= 1:
                        pl = lts[i - 1]
                        eng.wait_ge(sV.h, v10_0 + 15 * (i - 1) + 4)
                        eng.activation(q_r[:, pl:pl + 1], q_r[:, pl:pl + 1],
                                       AF.Ln, bias=eps_col[:])
                        eng.activation(q_r[:, pl:pl + 1], q_r[:, pl:pl + 1],
                                       AF.Exp, scale=-0.5)
                    else:
                        eng.pad(2)
                    eng.wait_ge(sV.h, v10_0 + 15 * (i - 1) + 13)
                    eng.activation(s10[:], x2t, AF.Identity,
                                   accum_out=rms_ms[:, lt:lt + 1])
                    eng.activation(s10[:], x2t, AF.Square,
                                   accum_out=rms_rs[:, lt:lt + 1])
                pl = lts[3]
                eng.wait_ge(sV.h, v10_0 + 15 * 3 + 4)
                eng.activation(q_r[:, pl:pl + 1], q_r[:, pl:pl + 1],
                               AF.Ln, bias=eps_col[:])
                eng.activation(q_r[:, pl:pl + 1], q_r[:, pl:pl + 1],
                               AF.Exp, scale=-0.5)
            S(p10_scalar)

            def p10_vec(eng, HF=HF, lts=lts, s10_0=s10_0):
                for i, lt in enumerate(lts):
                    x2t = x2[:, lt * HID:(lt + 1) * HID]
                    eng.wait_ge(sS.h, s10_0 + 4 * i + 4)
                    eng.tensor_scalar(q_m[:, lt:lt + 1], rms_ms[:, lt:lt + 1],
                                      1.0 / HID, None, ALU.mult)
                    eng.tensor_scalar(q_r[:, lt:lt + 1], rms_rs[:, lt:lt + 1],
                                      1.0 / HID, None, ALU.mult)
                    eng.tensor_tensor(q_am[:, lt:lt + 1], q_m[:, lt:lt + 1],
                                      q_m[:, lt:lt + 1], ALU.mult)
                    eng.tensor_sub(q_r[:, lt:lt + 1], q_r[:, lt:lt + 1],
                                   q_am[:, lt:lt + 1])
                    eng.tensor_scalar(s10[:], x2t, q_m[:, lt:lt + 1], None,
                                      ALU.subtract)
                    eng.reduce_max(q_am[:, lt:lt + 1], s10[:], AX,
                                   apply_absolute_value=True)
                    eng.wait_ge(sS.h, lnexp_cnt(i))
                    eng.tensor_tensor(q_am[:, lt:lt + 1], q_am[:, lt:lt + 1],
                                      q_r[:, lt:lt + 1], ALU.mult)
                    eng.tensor_scalar_max(q_am[:, lt:lt + 1],
                                          q_am[:, lt:lt + 1], EPS)
                    eng.reciprocal(q_sa[:, lt:lt + 1], q_am[:, lt:lt + 1])
                    eng.tensor_scalar(q_sa[:, lt:lt + 1], q_sa[:, lt:lt + 1],
                                      127.0, None, ALU.mult)
                    eng.tensor_tensor(fq1[:, lt:lt + 1], q_r[:, lt:lt + 1],
                                      q_sa[:, lt:lt + 1], ALU.mult)
                    eng.tensor_scalar(s10[:], s10[:], fq1[:, lt:lt + 1], None,
                                      ALU.mult)
                    eng.tensor_scalar(xqT[:, i * HID:(i + 1) * HID], s10[:],
                                      MAGIC, MAGIC, ALU.add, ALU.subtract)
                    eng.tensor_tensor(fq1[:, lt:lt + 1], q_am[:, lt:lt + 1],
                                      qthr[:, 6:7], ALU.mult)
                    eng.tensor_tensor(fq3[:, lt:lt + 1], q_am[:, lt:lt + 1],
                                      qthr[:, 7:8], ALU.mult)
            V(p10_vec)
            barrier()

            t11_0 = sT.n; s11_0 = sS.n

            def p11_tensor(eng, t11_0=t11_0, s11_0=s11_0):
                for i in range(4):
                    for kh in range(2):
                        idx = i * 2 + kh
                        eng.fence()
                        if idx >= 2:
                            eng.wait_ge(sS.h, s11_0 + idx - 1)
                        pv = pbig_b[:, (idx % 2) * 1024:((idx % 2) + 1) * 1024]
                        for kk in range(8):
                            k = kh * 8 + kk
                            eng.transpose(
                                pv[:, kk * 128:(kk + 1) * 128],
                                xqT[:, i * HID + k * 128:i * HID + (k + 1) * 128],
                                identb[:])
            TE(p11_tensor)

            def p11_scalar(eng, t11_0=t11_0):
                for i in range(4):
                    for kh in range(2):
                        idx = i * 2 + kh
                        eng.wait_ge(sT.h, t11_0 + 8 * (idx + 1))
                        k0 = kh * 8
                        eng.copy(
                            xqF[:].rearrange("p (k t) -> p k t", k=16)
                            [:, k0:k0 + 8, i * 128:(i + 1) * 128],
                            pbig_b[:, (idx % 2) * 1024:((idx % 2) + 1) * 1024]
                            .rearrange("p (k t) -> p k t", k=8))
            S(p11_scalar)
            barrier()

            build_bcast([fq1[:, lt:lt + 1] for lt in lts], RS1[:])
            barrier()

            t12_0 = sT.n; v12_0 = sV.n; s12_0 = sS.n
            LA, LB = sLdA.n, sLdB.n
            STb = sSt.n
            vb0 = v12_0 + 2

            if HF == 0:
                CA0, CB0 = sCastA.n, sCastB.n
                g12_0 = sG.n
                vtmp = vb(7680, 512)
                vtmpV = vb(OFF_D + 5120, 512)

                def silu_cnt(j, s0=s12_0):
                    return s0 + 7 * (j + 1) + 7 if j < NHT - 1 else s0 + 7 * NHT + 1

                def p12b_sync(eng, g0=g12_0, v0=vb0, s0=s12_0, ST0=STb):
                    for ht in range(NHT):
                        if ht >= 1:
                            eng.wait_ge(sV.h, v0 + 7 * (ht - 1) + 2)
                        sLdA.dma(eng.dma_start(
                            out=wring4[0][:],
                            in_=I["w1t"][ht].rearrange("i k j -> i (k j)")
                            [:, 0:1024]))
                        if ht >= 1:
                            eng.wait_ge(sS.h, s0 + 7 * (ht - 1) + 6)
                        sLdB.dma(eng.dma_start(
                            out=wring4[1][:],
                            in_=I["w1t"][ht].rearrange("i k j -> i (k j)")
                            [:, 1024:2048]))
                        if ht >= 1:
                            eng.wait_ge(sSt.h, ST0 + 32 * ht)
                        eng.wait_ge(sV.h, v0 + 7 * ht + 3)
                        eng.wait_ge(sG.h, g0 + 3 * ht + 3)
                        sSt.dma(eng.dma_start(out=th1[ht], in_=t1bR[ht % 2][:]))
                        sSt.dma(eng.dma_start(out=th3[ht], in_=t3bR[ht % 2][:]))
                SY(p12b_sync)

                def p12b_gp(eng, g0=g12_0, s0=s12_0, t0=t12_0, ST0=STb):
                    for ht in range(NHT):
                        for jj in range(2):
                            if ht >= 1:
                                eng.wait_ge(sS.h, s0 + 7 * (ht - 1) + 2 * (jj + 1))
                            (sCastA, sCastB)[jj].dma(eng.dma_start(
                                out=wring4[2 + jj][:],
                                in_=I["w3t"][ht].rearrange("i k j -> i (k j)")
                                [:, jj * 1024:(jj + 1) * 1024]))
                        for half in range(2):
                            eng.wait_ge(sS.h, s0 + 7 * ht + 2 * (half + 1))
                            if half == 0 and ht >= 2:
                                eng.wait_ge(sT.h, t0 + 2 * (ht - 2) + 2)
                                eng.wait_ge(sSt.h, ST0 + 32 * (ht - 1))
                            eng.tensor_add(
                                t3bR[ht % 2][:, half * 1024:(half + 1) * 1024],
                                sg4[2 * half][:], sg4[2 * half + 1][:])
                        eng.wait_ge(sS.h, s0 + 7 * ht + 6)
                        eng.tensor_add(t1bR[ht % 2][:, 1024:2048],
                                       t1bR[ht % 2][:, 1024:2048], vtmp[:])
                G(p12b_gp)

                def p12b_scalar(eng, g0=g12_0, s0=s12_0, v0=vb0, t0=t12_0,
                                ST0=STb, CA0=CA0, CB0=CB0, LB=LB):
                    for ht in range(NHT):
                        for half in range(2):
                            eng.wait_ge((sCastA, sCastB)[half].h,
                                        (CA0, CB0)[half] + 16 * (ht + 1))
                            if ht >= 1:
                                eng.wait_ge(sG.h, g0 + 3 * (ht - 1) + half + 1)
                            w = wring4[2 + half]
                            eng.activation(sg4[2 * half][:], w[:], AF.Sign,
                                           bias=qthr[:, 5:6])
                            eng.activation(sg4[2 * half + 1][:], w[:],
                                           AF.Sign, bias=qthr[:, 2:3])
                        eng.wait_ge(sLdB.h, LB + 16 * (ht + 1))
                        if ht >= 1:
                            eng.wait_ge(sG.h, g0 + 3 * (ht - 1) + 3)
                        if ht >= 2:
                            eng.wait_ge(sT.h, t0 + 2 * (ht - 2) + 1)
                            eng.wait_ge(sSt.h, ST0 + 32 * (ht - 1))
                        eng.activation(t1bR[ht % 2][:, 1024:2048], wring4[1][:],
                                       AF.Sign, bias=qthr[:, 3:4])
                        eng.activation(vtmp[:], wring4[1][:],
                                       AF.Sign, bias=qthr[:, 0:1])
                        if ht >= 1:
                            eng.wait_ge(sV.h, v0 + 7 * ht + 4)
                            eng.activation(scrF2[:], scrF[:], AF.Silu)
                        else:
                            eng.pad(1)
                    eng.wait_ge(sV.h, v0 + 7 * NHT + 1)
                    eng.activation(scrF2[:], scrF[:], AF.Silu)
                S(p12b_scalar)

                def p12b_vec(eng, v0=vb0, s0=s12_0, t0=t12_0, ST0=STb,
                             LA=LA):
                    eng.memset(amax_acc[:], 0.0)
                    eng.memset(amin_acc[:], 0.0)

                    def evac(ht):
                        b1 = (pA0, pB0)[ht % 2]; b3 = (pA1, pB1)[ht % 2]
                        eng.wait_ge(sT.h, t0 + 2 * ht + 1)
                        if ht >= 1:
                            eng.wait_ge(sS.h, silu_cnt(ht - 1))
                        eng.tensor_tensor(scrF[:], b1[:], RS1[:], ALU.mult)
                        eng.wait_ge(sS.h, silu_cnt(ht))
                        eng.wait_ge(sT.h, t0 + 2 * ht + 2)
                        eng.tensor_tensor(hbuf[:, ht * 512:(ht + 1) * 512],
                                          scrF2[:], b3[:], ALU.mult)
                        eng.tensor_max(amax_acc[:], amax_acc[:],
                                       hbuf[:, ht * 512:(ht + 1) * 512])
                        eng.tensor_tensor(amin_acc[:], amin_acc[:],
                                          hbuf[:, ht * 512:(ht + 1) * 512],
                                          ALU.min)

                    for ht in range(NHT):
                        eng.wait_ge(sLdA.h, LA + 16 * (ht + 1))
                        if ht >= 2:
                            eng.wait_ge(sT.h, t0 + 2 * (ht - 2) + 1)
                            eng.wait_ge(sSt.h, ST0 + 32 * (ht - 1))
                        w = wring4[0]
                        eng.tensor_scalar(t1bR[ht % 2][:, 0:1024], w[:],
                                          qthr[:, 0:1], 2.0, ALU.is_gt,
                                          ALU.mult)
                        eng.tensor_scalar(vtmpV[:], w[:], qthr[:, 3:4],
                                          2.0, ALU.is_lt, ALU.mult)
                        eng.tensor_sub(t1bR[ht % 2][:, 0:1024],
                                       t1bR[ht % 2][:, 0:1024], vtmpV[:])
                        if ht >= 1:
                            evac(ht - 1)
                        else:
                            eng.pad(4)
                    evac(NHT - 1)
                V(p12b_vec)

                def p12b_tensor(eng, g0=g12_0, v0=vb0):
                    for ht in range(NHT):
                        b1 = (pA0, pB0)[ht % 2]; b3 = (pA1, pB1)[ht % 2]
                        eng.fence()
                        eng.wait_ge(sV.h, v0 + 7 * ht + 3)
                        eng.wait_ge(sG.h, g0 + 3 * ht + 3)
                        if ht >= 2:
                            eng.wait_ge(sV.h, v0 + 7 * (ht - 1) + 4)
                        for k in range(16):
                            eng.matmul(b1[:],
                                       t1bR[ht % 2][:, k * 128:(k + 1) * 128],
                                       xqF[:, k * 512:(k + 1) * 512],
                                       start=(k == 0), stop=(k == 15))
                        eng.fence()
                        eng.wait_ge(sG.h, g0 + 3 * ht + 2)
                        if ht >= 2:
                            eng.wait_ge(sV.h, v0 + 7 * (ht - 1) + 5)
                        for k in range(16):
                            eng.matmul(b3[:],
                                       t3bR[ht % 2][:, k * 128:(k + 1) * 128],
                                       xqF[:, k * 512:(k + 1) * 512],
                                       start=(k == 0), stop=(k == 15))
                TE(p12b_tensor)
            else:
                stp2 = sSt.n
                CA1, CB1 = sCastA.n, sCastB.n

                def p12b_sync(eng, t0=t12_0, stp2=stp2):
                    eng.wait_ge(sSt.h, stp2)
                    for ht in range(NHT):
                        if ht >= 2:
                            eng.wait_ge(sT.h, t0 + 2 * (ht - 2) + 1)
                        (sLdA, sLdB)[ht % 2].dma(
                            eng.dma_start(out=t1bR[ht % 2][:], in_=th1[ht]))
                SY(p12b_sync)

                def p12b_gp1(eng, t0=t12_0, stp2=stp2):
                    eng.wait_ge(sSt.h, stp2)
                    for ht in range(NHT):
                        if ht >= 2:
                            eng.wait_ge(sT.h, t0 + 2 * (ht - 2) + 2)
                        (sCastA, sCastB)[ht % 2].dma(
                            eng.dma_start(out=t3bR[ht % 2][:], in_=th3[ht]))
                G(p12b_gp1)

                def p12b_vec(eng, v0=vb0, s0=s12_0, t0=t12_0):
                    eng.memset(amax_acc[:], 0.0)
                    eng.memset(amin_acc[:], 0.0)
                    for ht in range(NHT):
                        b1 = (pA0, pB0)[ht % 2]; b3 = (pA1, pB1)[ht % 2]
                        eng.wait_ge(sT.h, t0 + 2 * ht + 1)
                        if ht >= 1:
                            eng.wait_ge(sS.h, s0 + ht)
                        eng.tensor_tensor(scrF[:], b1[:], RS1[:], ALU.mult)
                        eng.wait_ge(sS.h, s0 + ht + 1)
                        eng.wait_ge(sT.h, t0 + 2 * ht + 2)
                        eng.tensor_tensor(hbuf[:, ht * 512:(ht + 1) * 512],
                                          scrF2[:], b3[:], ALU.mult)
                        eng.tensor_max(amax_acc[:], amax_acc[:],
                                       hbuf[:, ht * 512:(ht + 1) * 512])
                        eng.tensor_tensor(amin_acc[:], amin_acc[:],
                                          hbuf[:, ht * 512:(ht + 1) * 512],
                                          ALU.min)
                V(p12b_vec)

                def p12b_scalar(eng, v0=vb0):
                    for ht in range(NHT):
                        eng.wait_ge(sV.h, v0 + 4 * ht + 1)
                        eng.activation(scrF2[:], scrF[:], AF.Silu)
                S(p12b_scalar)

                def p12b_tensor(eng, v0=vb0, LA=LA, LB=LB,
                                CA1=CA1, CB1=CB1):
                    for ht in range(NHT):
                        L1 = t1bR[ht % 2]
                        L3 = t3bR[ht % 2]
                        b1 = (pA0, pB0)[ht % 2]; b3 = (pA1, pB1)[ht % 2]
                        eng.fence()
                        eng.wait_ge((sLdA, sLdB)[ht % 2].h,
                                    (LA, LB)[ht % 2] + 16 * (ht // 2 + 1))
                        if ht >= 2:
                            eng.wait_ge(sV.h, v0 + 4 * (ht - 2) + 1)
                        for k in range(16):
                            eng.matmul(b1[:], L1[:, k * 128:(k + 1) * 128],
                                       xqF[:, k * 512:(k + 1) * 512],
                                       start=(k == 0), stop=(k == 15))
                        eng.fence()
                        eng.wait_ge((sCastA, sCastB)[ht % 2].h,
                                    (CA1, CB1)[ht % 2] + 16 * (ht // 2 + 1))
                        if ht >= 2:
                            eng.wait_ge(sV.h, v0 + 4 * (ht - 2) + 2)
                        for k in range(16):
                            eng.matmul(b3[:], L3[:, k * 128:(k + 1) * 128],
                                       xqF[:, k * 512:(k + 1) * 512],
                                       start=(k == 0), stop=(k == 15))
                TE(p12b_tensor)
            barrier()

            tc0 = sT.n

            def p12c_tensor(eng):
                eng.fence()
                for i in range(4):
                    eng.transpose(pbig[:, i * 128:(i + 1) * 128],
                                  amax_acc[:, i * 128:(i + 1) * 128], identf[:])
                    eng.transpose(pbig[:, 512 + i * 128:512 + (i + 1) * 128],
                                  amin_acc[:, i * 128:(i + 1) * 128], identf[:])
                eng.fence()
            TE(p12c_tensor)
            tc1 = sT.n

            def p12c_vec(eng, lts=lts, tc1=tc1):
                eng.wait_ge(sT.h, tc1)
                for i, lt in enumerate(lts):
                    eng.tensor_scalar(qv_scr[:, 0:128],
                                      pbig[:, 512 + i * 128:512 + (i + 1) * 128],
                                      -1.0, None, ALU.mult)
                    eng.tensor_max(qv_scr[:, 0:128], qv_scr[:, 0:128],
                                   pbig[:, i * 128:(i + 1) * 128])
                    eng.reduce_max(rms_ms[:, lt:lt + 1],
                                   qv_scr[:, 0:128], AX)
                    eng.tensor_tensor(rms_rs[:, lt:lt + 1], rms_ms[:, lt:lt + 1],
                                      fq3[:, lt:lt + 1], ALU.mult)
                    eng.tensor_scalar_max(rms_rs[:, lt:lt + 1],
                                          rms_rs[:, lt:lt + 1], EPS)
                    eng.reciprocal(q_sa[:, lt:lt + 1], rms_rs[:, lt:lt + 1])
                    eng.tensor_scalar(q_sa[:, lt:lt + 1], q_sa[:, lt:lt + 1],
                                      127.0, None, ALU.mult)
                    eng.tensor_tensor(fq1[:, lt:lt + 1], fq3[:, lt:lt + 1],
                                      q_sa[:, lt:lt + 1], ALU.mult)
                    eng.tensor_tensor(fq3[:, lt:lt + 1], rms_rs[:, lt:lt + 1],
                                      qthr[:, 8:9], ALU.mult)
            V(p12c_vec)

            build_bcast([fq1[:, lt:lt + 1] for lt in lts], SCLH[:])
            build_bcast([fq3[:, lt:lt + 1] for lt in lts], RS2[:])

            def p12c_quant(eng):
                for cq in range(11):
                    eng.tensor_tensor(
                        qv_scr[:].rearrange("p (a t) -> p a t", a=4),
                        hbuf[:, cq * 2048:(cq + 1) * 2048]
                        .rearrange("p (a t) -> p a t", a=4),
                        SCLH[:][:, None, :].broadcast_to([128, 4, 512]),
                        ALU.mult)
                    eng.tensor_scalar(hbuf[:, cq * 2048:(cq + 1) * 2048],
                                      qv_scr[:], MAGIC, MAGIC,
                                      ALU.add, ALU.subtract)
            V(p12c_quant)
            barrier()

            t12d = sT.n; v12d = sV.n; s12d = sS.n
            LA2, LB2 = sLdA.n, sLdB.n
            CA2 = sCastA.n
            STd = sSt.n
            ksplit = [(0, 16), (16, 32), (32, 44)]

            if HF == 1:
                vA = sV.n

                def pstoreA(eng, vA=vA):
                    eng.wait_ge(sV.h, vA)
                    for lt in range(4):
                        sSt.dma(eng.dma_start(
                            out=out_h[lt * 128:(lt + 1) * 128, :],
                            in_=x2[:, lt * HID:(lt + 1) * HID]))
                SY(pstoreA)

            if HF == 0:
                gd0 = sG.n
                CA0, CB0 = sCastA.n, sCastB.n
                vtmpD = vb(OFF_A, 1024)
                sgnw4 = [vb(OFF_A + 1024, 1024), vb(OFF_A + 2048, 1024),
                         vb(OFF_A + 3072, 1024), vb(7168, 1024)]
                wbJ = [vf(OFF_D + 1024 + j * 2048, 2048) for j in range(3)]
                KSL = [(0, 2048), (2048, 4096), (4096, 5632)]

                def p12d_sync(eng, g0=gd0, v0=v12d, ST0=STd):
                    eng.wait_ge(sSt.h, ST0)
                    for d in range(16):
                        if d >= 1:
                            eng.wait_ge(sV.h, v0 + 5 * (d - 1) + 2)
                        sLdA.dma(eng.dma_start(
                            out=wbJ[0][:],
                            in_=I["w2t"][d].rearrange("i k j -> i (k j)")
                            [:, 0:2048]))
                        if d >= 1:
                            eng.wait_ge(sSt.h, ST0 + 16 * d)
                        eng.wait_ge(sV.h, v0 + 5 * d + 3)
                        eng.wait_ge(sG.h, g0 + 2 * d + 2)
                        sSt.dma(eng.dma_start(out=th2[d], in_=w2tR[d % 2][:]))
                SY(p12d_sync)

                def p12d_gp(eng, g0=gd0, s0=s12d, t0=t12d, ST0=STd):
                    for d in range(16):
                        for jp in range(2):
                            if d >= 1:
                                eng.wait_ge(sS.h, s0 + 4 * (d - 1) + 2 * (jp + 1))
                            (sCastA, sCastB)[jp].dma(eng.dma_start(
                                out=wbJ[1 + jp][0:128, 0:KSL[1 + jp][1]
                                                - KSL[1 + jp][0]],
                                in_=I["w2t"][d].rearrange("i k j -> i (k j)")
                                [:, KSL[1 + jp][0]:KSL[1 + jp][1]]))
                        for jp in range(2):
                            eng.wait_ge(sS.h, s0 + 4 * d + 2 * (jp + 1))
                            if jp == 0:
                                if d >= 2:
                                    eng.wait_ge(sT.h, t0 + 5 * (d - 2) + 5)
                                eng.wait_ge(sSt.h,
                                            ST0 + (16 * (d - 1) if d >= 2
                                                   else 0))
                            a, b = KSL[1 + jp]
                            eng.tensor_add(w2tR[d % 2][:, a:b],
                                           sgnw4[2 * jp][0:128, 0:b - a],
                                           sgnw4[2 * jp + 1][0:128, 0:b - a])
                G(p12d_gp)

                def p12d_scalar(eng, g0=gd0, s0=s12d, CA0=CA0, CB0=CB0):
                    for d in range(16):
                        for jp in range(2):
                            eng.wait_ge((sCastA, sCastB)[jp].h,
                                        (CA0, CB0)[jp] + 16 * (d + 1))
                            if d >= 1:
                                eng.wait_ge(sG.h, g0 + 2 * (d - 1) + jp + 1)
                            a, b = KSL[1 + jp]
                            w = wbJ[1 + jp][0:128, 0:b - a]
                            eng.activation(sgnw4[2 * jp][0:128, 0:b - a], w,
                                           AF.Sign, bias=qthr[:, 4:5])
                            eng.activation(sgnw4[2 * jp + 1][0:128, 0:b - a],
                                           w, AF.Sign, bias=qthr[:, 1:2])
                S(p12d_scalar)

                def p12d_vec(eng, v0=v12d, s0=s12d, t0=t12d, ST0=STd,
                             LA2=LA2):
                    def evac(d):
                        eng.wait_ge(sT.h, t0 + 5 * d + 5)
                        eng.tensor_tensor(ffn_scr[:],
                                          (pA0, pB0)[d % 2][:], RS2[:],
                                          ALU.mult)
                        eng.wait_ge(sT.h, t0 + 5 * (d + 1) + 4)
                        eng.tensor_add(
                            x2[:].rearrange("p (l f) -> p l f", l=NLT)
                            [:, HF * 4:HF * 4 + 4, d * 128:(d + 1) * 128],
                            x2[:].rearrange("p (l f) -> p l f", l=NLT)
                            [:, HF * 4:HF * 4 + 4, d * 128:(d + 1) * 128],
                            pbig[:, 0:512].rearrange("p (l f) -> p l f", l=4))

                    for d in range(16):
                        eng.wait_ge(sLdA.h, LA2 + 16 * (d + 1))
                        if d >= 2:
                            eng.wait_ge(sT.h, t0 + 5 * (d - 2) + 5)
                        eng.wait_ge(sSt.h,
                                    ST0 + (16 * (d - 1) if d >= 2 else 0))
                        eng.tensor_scalar(w2tR[d % 2][:, 0:2048], wbJ[0][:],
                                          qthr[:, 1:2], 2.0, ALU.is_gt,
                                          ALU.mult)
                        eng.tensor_scalar(vtmpD[:], wbJ[0][:], qthr[:, 4:5],
                                          2.0, ALU.is_lt, ALU.mult)
                        eng.tensor_sub(w2tR[d % 2][:, 0:2048],
                                       w2tR[d % 2][:, 0:2048], vtmpD[:])
                        if d >= 1:
                            evac(d - 1)
                        else:
                            eng.pad(2)
                    evac(15)
                V(p12d_vec)

                def p12d_tensor(eng, g0=gd0, v0=v12d):
                    for d in range(16):
                        bank = (pA0, pB0)[d % 2]
                        eng.fence()
                        eng.wait_ge(sV.h, v0 + 5 * d + 4)
                        if d >= 1:
                            for i in range(4):
                                eng.transpose(
                                    pbig[:, i * 128:(i + 1) * 128],
                                    ffn_scr[:, i * 128:(i + 1) * 128],
                                    identf[:])
                        else:
                            eng.pad(4)
                        eng.fence()
                        eng.wait_ge(sG.h, g0 + 2 * d + 2)
                        for k in range(NHT):
                            eng.matmul(bank[:],
                                       w2tR[d % 2][:, k * 128:(k + 1) * 128],
                                       hbuf[:, k * 512:(k + 1) * 512],
                                       start=(k == 0), stop=(k == NHT - 1))
                    eng.fence()
                    eng.wait_ge(sV.h, v0 + 5 * 16 + 1)
                    for i in range(4):
                        eng.transpose(pbig[:, i * 128:(i + 1) * 128],
                                      ffn_scr[:, i * 128:(i + 1) * 128],
                                      identf[:])
                TE(p12d_tensor)
            else:
                def p12d_sync(eng, t0=t12d):
                    for d in range(16):
                        wt = w2tR[d % 2]
                        if d >= 2:
                            eng.wait_ge(sT.h, t0 + 5 * (d - 2) + 1)
                        sem = (sLdA, sLdB)[d % 2]
                        sem.dma(eng.dma_start(out=wt[:, 0:4096],
                                              in_=th2[d][:, 0:4096]))
                        sem.dma(eng.dma_start(out=wt[:, 4096:5632],
                                              in_=th2[d][:, 4096:5632]))
                SY(p12d_sync)

                def p12d_vec(eng, v0=v12d, t0=t12d):
                    for d in range(16):
                        eng.wait_ge(sT.h, t0 + 5 * d + 1)
                        eng.tensor_tensor(ffn_scr[:],
                                          (pA0, pB0)[d % 2][:], RS2[:],
                                          ALU.mult)
                        eng.wait_ge(sT.h, t0 + 5 * d + 5)
                        eng.tensor_add(
                            x2[:].rearrange("p (l f) -> p l f", l=NLT)
                            [:, HF * 4:HF * 4 + 4, d * 128:(d + 1) * 128],
                            x2[:].rearrange("p (l f) -> p l f", l=NLT)
                            [:, HF * 4:HF * 4 + 4, d * 128:(d + 1) * 128],
                            pbig[:, 0:512].rearrange("p (l f) -> p l f", l=4))
                V(p12d_vec)

                def p12d_tensor(eng, v0=v12d, LA2=LA2, LB2=LB2):
                    for d in range(16):
                        wt = w2tR[d % 2]
                        bank = (pA0, pB0)[d % 2]
                        eng.fence()
                        eng.wait_ge((sLdA, sLdB)[d % 2].h,
                                    (LA2, LB2)[d % 2] + 32 * (d // 2 + 1))
                        if d >= 2:
                            eng.wait_ge(sV.h, v0 + 2 * (d - 2) + 1)
                        for k in range(NHT):
                            eng.matmul(bank[:],
                                       wt[:, k * 128:(k + 1) * 128],
                                       hbuf[:, k * 512:(k + 1) * 512],
                                       start=(k == 0), stop=(k == NHT - 1))
                        eng.fence()
                        eng.wait_ge(sV.h, v0 + 2 * d + 1)
                        for i in range(4):
                            eng.transpose(pbig[:, i * 128:(i + 1) * 128],
                                          ffn_scr[:, i * 128:(i + 1) * 128],
                                          identf[:])
                TE(p12d_tensor)
            barrier()

        fv = sV.n

        def fin_sync(eng):
            eng.wait_ge(sV.h, fv)
            for lt in range(4, NLT):
                sSt.dma(eng.dma_start(out=out_h[lt * 128:(lt + 1) * 128, :],
                                      in_=x2[:, lt * HID:(lt + 1) * HID]))
            eng.wait_ge(sSt.h, sSt.n)
        SY(fin_sync)

    return nc


def kernel(**inputs):
    shared = _prep_shared(inputs)
    maps = [_prep_core(inputs, shared, c) for c in range(NCORE)]
    nc = build_nc()
    res = run_bass_kernel_spmd(nc, maps, core_ids=list(range(NCORE)))
    out = np.concatenate([res.results[c]["out"] for c in range(NCORE)], 0)
    return out[None]


# version 13 (final)
# speedup vs baseline: 1.0404x; 1.0404x over previous
import numpy as np
import ml_dtypes
from contextlib import ExitStack

import concourse.bass as bass
import concourse.mybir as mybir
from concourse.bass_utils import run_bass_kernel_spmd

F32 = mybir.dt.float32
BF16 = mybir.dt.bfloat16
AF = mybir.ActivationFunctionType
ALU = mybir.AluOpType
AX = mybir.AxisListType.X

NCORE = 8
T = 1024
NLT = 8
LC = 128
NCH = 8
HID = 2048
INTER = 4096
NHB = 35
H = 64
PD = 64
NS = 128
FFH = 5632
NHT = 44
EPS = 1e-5
MAGIC = 12582912.0
WCNT = float(5632 * 2048)

_DEBUG_WANT = set()


def _prep_shared(ins):
    W = np.ascontiguousarray(ins["L0_attn_in_proj_w"], dtype=np.float32)
    b_in = np.zeros(8576, np.float32)
    b_in[:8512] = np.asarray(ins["L0_attn_in_proj_b"], np.float32)
    Wh = np.zeros((2048, NHB * 128), np.float32)
    Wh[:, :4416] = W[:, 4096:8512]
    d = {}
    d["w_hbc"] = np.ascontiguousarray(
        Wh.reshape(16, 128, NHB, 128).transpose(2, 1, 0, 3))
    d["w_gate"] = np.ascontiguousarray(
        W[:, :4096].reshape(16, 128, 8, 512).transpose(2, 1, 0, 3))
    d["b_cols"] = np.ascontiguousarray(b_in.reshape(67, 128).T)
    d["b_gate_row"] = np.ascontiguousarray(b_in[None, :4096])
    d["convw"] = np.ascontiguousarray(
        np.asarray(ins["L0_attn_conv_w"], np.float32).reshape(34, 128, 4)
        .transpose(1, 0, 2).reshape(128, 136))
    d["dtb"] = np.asarray(ins["L0_attn_dt_bias"], np.float32).reshape(64, 1).copy()
    d["alog"] = np.asarray(ins["L0_attn_A_log"], np.float32).reshape(64, 1).copy()
    d["d_row"] = np.asarray(ins["L0_attn_D"], np.float32).reshape(1, 64).copy()
    d["gam"] = np.ascontiguousarray(
        np.asarray(ins["L0_attn_norm_weight"], np.float32).reshape(32, 128).T)
    Wo = np.ascontiguousarray(ins["L0_attn_out_proj_w"], dtype=np.float32)
    d["w_out"] = np.ascontiguousarray(
        Wo.reshape(32, 128, 16, 128).transpose(2, 1, 0, 3))
    d["b_out_cols"] = np.ascontiguousarray(
        np.asarray(ins["L0_attn_out_proj_b"], np.float32).reshape(16, 128).T)
    w1 = np.ascontiguousarray(ins["L0_ffn_w1"], dtype=np.float32)
    w2 = np.ascontiguousarray(ins["L0_ffn_w2"], dtype=np.float32)
    w3 = np.ascontiguousarray(ins["L0_ffn_w3"], dtype=np.float32)
    d["w1t"] = np.ascontiguousarray(w1.reshape(NHT, 128, 16, 128).transpose(0, 3, 2, 1))
    d["w3t"] = np.ascontiguousarray(w3.reshape(NHT, 128, 16, 128).transpose(0, 3, 2, 1))
    d["w2t"] = np.ascontiguousarray(w2.reshape(16, 128, NHT, 128).transpose(0, 3, 2, 1))
    d["identf"] = np.eye(128, dtype=np.float32)
    d["identb"] = np.eye(128, dtype=ml_dtypes.bfloat16)
    d["ones_row"] = np.ones((1, 128), np.float32)
    d["ones_col"] = np.ones((128, 1), np.float32)
    d["trimask"] = np.triu(np.ones((128, 128), np.float32))
    d["_w1"], d["_w2"], d["_w3"] = w1, w2, w3
    return d


def _prep_core(ins, shared, c):
    x_full = np.ascontiguousarray(np.asarray(ins["L0_attn_hidden_states"])[0],
                                  dtype=np.float32)
    lo = c * T
    d = {k: v for k, v in shared.items() if not k.startswith("_")}
    d["x"] = np.ascontiguousarray(x_full[lo:lo + T])
    if c == 0:
        d["halo_x"] = np.zeros((3, HID), np.float32)
        d["halo_mask"] = np.zeros((128, 1), np.float32)
    else:
        d["halo_x"] = np.ascontiguousarray(x_full[lo - 3:lo])
        d["halo_mask"] = np.ones((128, 1), np.float32)
    sm = np.zeros((128, 7), np.float32)
    sm[:, :c] = 1.0
    d["smask"] = sm
    for nm, w in (("ws1", shared["_w1"]), ("ws2", shared["_w2"]), ("ws3", shared["_w3"])):
        fl = w.reshape(-1)
        n = fl.size // NCORE
        d[nm] = np.ascontiguousarray(fl[c * n:(c + 1) * n].reshape(128, n // 128))
    return d


class Sem:
    def __init__(self, h):
        self.h, self.n = h, 0

    def bump(self, instr, k=1):
        instr.then_inc(self.h, k)
        self.n += k
        return self.n

    def dma(self, instr):
        return self.bump(instr, 16)


def build_nc():
    nc = bass.Bass("TRN2", target_bir_lowering=False)
    I = {}

    def inp(name, shape, dt=F32):
        I[name] = nc.dram_tensor(name, list(shape), dt, kind="ExternalInput").ap()

    inp("x", [T, HID]); inp("halo_x", [3, HID]); inp("halo_mask", [128, 1])
    inp("smask", [128, 7])
    inp("w_hbc", [NHB, 128, 16, 128]); inp("w_gate", [8, 128, 16, 512])
    inp("b_cols", [128, 67]); inp("b_gate_row", [1, 4096]); inp("convw", [128, 136])
    inp("dtb", [64, 1]); inp("alog", [64, 1]); inp("d_row", [1, 64])
    inp("gam", [128, 32]); inp("w_out", [16, 128, 32, 128]); inp("b_out_cols", [128, 16])
    inp("w1t", [NHT, 128, 16, 128]); inp("w3t", [NHT, 128, 16, 128])
    inp("w2t", [16, 128, NHT, 128])
    inp("ws1", [128, 11264]); inp("ws2", [128, 11264]); inp("ws3", [128, 11264])
    inp("identf", [128, 128]); inp("identb", [128, 128], BF16)
    inp("ones_row", [1, 128]); inp("ones_col", [128, 1])
    inp("trimask", [128, 128])

    out_h = nc.dram_tensor("out", [T, HID], F32, kind="ExternalOutput").ap()
    cc_in = nc.dram_tensor("cc_in", [128, 4164], F32, kind="Internal").ap()
    cc_out = nc.dram_tensor("cc_out", [NCORE * 128, 4164], F32, kind="Internal").ap()
    th1 = nc.dram_tensor("th1", [NHT, 128, 2048], BF16, kind="Internal").ap()
    th3 = nc.dram_tensor("th3", [NHT, 128, 2048], BF16, kind="Internal").ap()
    th2 = nc.dram_tensor("th2", [16, 128, FFH], BF16, kind="Internal").ap()

    DBG = {}

    def dbgt(name, shape, dt=F32):
        if name in _DEBUG_WANT:
            DBG[name] = nc.dram_tensor("dbg_" + name, list(shape), dt,
                                       kind="ExternalOutput").ap()
            return DBG[name]
        return None

    ctx = ExitStack()
    with ctx:
        def sb(name, shape, dt=F32, st=None):
            return ctx.enter_context(nc.sbuf_tensor("sb_" + name, shape, dt))

        def psum(name, shape):
            return ctx.enter_context(nc.psum_tensor(name, shape, F32))

        pA0 = psum("pA0", [128, 512]); pA1 = psum("pA1", [128, 512])
        pB0 = psum("pB0", [128, 512]); pB1 = psum("pB1", [128, 512])
        pbig = psum("pbig", [128, 1024])
        pD = psum("pD", [128, 512]); pE = psum("pE", [128, 512])
        pbig_b = pbig[:].bitcast(BF16)
        pE_b = pE[:].bitcast(BF16)

        sLd = Sem(ctx.enter_context(nc.semaphore("sLd")))
        sLdA = Sem(ctx.enter_context(nc.semaphore("sLdA")))
        sLdB = Sem(ctx.enter_context(nc.semaphore("sLdB")))
        sCastA = Sem(ctx.enter_context(nc.semaphore("sCastA")))
        sCastB = Sem(ctx.enter_context(nc.semaphore("sCastB")))
        sCastC = Sem(ctx.enter_context(nc.semaphore("sCastC")))
        sSt = Sem(ctx.enter_context(nc.semaphore("sSt")))
        sT = Sem(ctx.enter_context(nc.semaphore("sT")))
        sV = Sem(ctx.enter_context(nc.semaphore("sV")))
        sS = Sem(ctx.enter_context(nc.semaphore("sS")))
        sG = Sem(ctx.enter_context(nc.semaphore("sG")))
        sBar = Sem(ctx.enter_context(nc.semaphore("sBar")))
        sCC = Sem(ctx.enter_context(nc.semaphore("sCC")))
        sWs = Sem(ctx.enter_context(nc.semaphore("sWs")))
        sWs2 = Sem(ctx.enter_context(nc.semaphore("sWs2")))

        block = ctx.enter_context(nc.Block())
        ENG = ("tensor", "vector", "scalar", "gpsimd", "sync")

        FENCE_OPS = frozenset((
            "tensor_scalar", "tensor_single_scalar", "tensor_tensor", "tensor_add",
            "tensor_sub", "tensor_mul", "tensor_max", "tensor_copy", "tensor_reduce",
            "reduce_max", "reduce_sum", "scalar_tensor_tensor", "tensor_tensor_scan",
            "memset", "memzero", "activation", "copy", "sign", "mul", "add",
            "square", "sqrt", "select", "copy_predicated", "reciprocal",
            "tensor_scalar_add", "tensor_scalar_sub", "tensor_scalar_mul",
            "tensor_scalar_max", "tensor_scalar_min", "tensor_relu", "iota"))

        class FEng:
            def __init__(self, eng, sem):
                self._eng, self._sem = eng, sem

            def __getattr__(self, name):
                a = getattr(self._eng, name)
                if name in FENCE_OPS:
                    def w(*args, **kw):
                        self._eng.wait_ge(self._sem.h, self._sem.n)
                        r = a(*args, **kw)
                        self._sem.bump(r)
                        return r
                    return w
                return a

            def pad(self, k=1):
                for _ in range(k):
                    self._eng.wait_ge(self._sem.h, self._sem.n)
                    self._sem.bump(self._eng.nop())

        def V(f):
            block.vector(lambda e: f(FEng(e, sV)))

        def S(f):
            block.scalar(lambda e: f(FEng(e, sS)))

        def G(f):
            block.gpsimd(lambda e: f(FEng(e, sG)))

        class TEng:
            def __init__(self, eng):
                self._eng = eng

            def matmul(self, *args, **kw):
                r = self._eng.matmul(*args, **kw)
                if kw.get("stop", True):
                    sT.bump(r)
                return r

            def transpose(self, *args, **kw):
                r = self._eng.transpose(*args, **kw)
                sT.bump(r)
                return r

            def pad(self, k=1):
                for _ in range(k):
                    self._eng.wait_ge(sT.h, sT.n)
                    sT.bump(self._eng.nop())

            def fence(self):
                self._eng.wait_ge(sT.h, sT.n)

            def __getattr__(self, name):
                return getattr(self._eng, name)

        def TE(f):
            block.tensor(lambda e: f(TEng(e)))

        def SY(f):
            block.sync(f)

        ESEM = {"tensor": sT, "vector": sV, "scalar": sS, "gpsimd": sG}

        def barrier():
            for e in ENG:
                def _b(eng, e=e):
                    if e in ESEM:
                        eng.wait_ge(ESEM[e].h, ESEM[e].n)
                    eng.nop().then_inc(sBar.h, 1)
                getattr(block, e)(_b)
            sBar.n += 5
            tgt = sBar.n
            for e in ENG:
                def _w(eng, tgt=tgt):
                    eng.wait_ge(sBar.h, tgt)
                getattr(block, e)(_w)

        def dbg_store(name, src_ap, shape, dt=F32):
            t = dbgt(name, shape, dt)
            if t is None:
                return
            def _st(eng, t=t, src_ap=src_ap):
                sSt.dma(eng.dma_start(out=t, in_=src_ap))
                eng.wait_ge(sSt.h, sSt.n)
            block.sync(_st)

        identb = sb("identb", [128, 128], BF16)
        identf = sb("identf", [128, 128])
        ones_row = sb("ones_row", [1, 128]); ones_col = sb("ones_col", [128, 1])
        trimask = sb("trimask", [128, 128])
        b_cols = sb("b_cols", [128, 67]); mb_cols = sb("mb_cols", [128, 35])
        convw = sb("convw", [128, 136])
        dtb = sb("dtb", [64, 1]); alog = sb("alog", [64, 1]); d_rowt = sb("d_rowt", [1, 64])
        gam = sb("gam", [128, 32]); b_out_cols = sb("b_out_cols", [128, 16])
        halo_mask = sb("halo_mask", [128, 1]); smask = sb("smask", [128, 7])
        A_col = sb("A_col", [64, 1]); dtbias2 = sb("dtbias2", [64, 1])
        eps_col = sb("eps_col", [128, 1])
        ln_m = sb("ln_m", [128, NLT]); ln_r = sb("ln_r", [128, NLT])
        wpart = sb("wpart", [128, 4]); wk = sb("wk", [128, 4])
        xn1hF = sb("xn1hF", [128, 16 * 3], BF16)
        dtraw = sb("dtraw", [64, T])
        B_F = sb("B_F", [128, T], BF16); C_F = sb("C_F", [128, T], BF16)
        B_Tt = sb("B_Tt", [128, NCH * 128], BF16)
        q_m = sb("q_m", [128, NLT]); q_r = sb("q_r", [128, NLT])
        q_am = sb("q_am", [128, NLT]); q_sa = sb("q_sa", [128, NLT])
        akrow = sb("akrow", [1, NCH * 64 + 128])
        dtil = sb("dtil", [128, 64]); Dtile = sb("Dtile", [128, 64])

        NMEGA = 48640
        MEGA = sb("MEGA", [128, NMEGA])
        MEGA_B = MEGA[:].bitcast(BF16)

        def vf(off, cols, rows=128):
            assert off + cols <= NMEGA
            return MEGA[0:rows, off:off + cols]

        def vb(off, cols_f32, rows=128):
            assert off + cols_f32 <= NMEGA
            return MEGA_B[0:rows, 2 * off:2 * (off + cols_f32)]

        OFF_A32, OFF_A, OFF_B, OFF_D = 0, 8192, 24576, 40960
        xn1F = vb(OFF_A32, 8192)
        hsT = vb(OFF_A, 16384)
        xgnF = vb(OFF_A, 16384)
        sgT = vb(OFF_B, 16384)
        x2 = vf(OFF_B, 16384)

        xT = vf(OFF_A, 16384)
        xn1T = vb(OFF_B, 8192)
        sp_scr = vf(OFF_B + 8192, 2048)
        xhT = vf(OFF_B + 10240, 2048, rows=3)
        xn1h = vb(OFF_B + 12288, 1024, rows=3)
        hstat = vf(OFF_B + 13312, 6, rows=3)
        wcol = vf(OFF_B + 13320, 12)
        ln_s = vf(OFF_B + 13336, NLT)
        ln_q = vf(OFF_B + 13344, NLT)
        wsl = [vf(OFF_D, 2816), vf(OFF_D + 2816, 2816)]

        def p0_sync(eng):
            for dst, src in (
                (identb, "identb"), (identf, "identf"), (ones_row, "ones_row"),
                (ones_col, "ones_col"), (trimask, "trimask"),
                (b_cols, "b_cols"), (convw, "convw"),
                (dtb, "dtb"), (alog, "alog"), (d_rowt, "d_row"), (gam, "gam"),
                (b_out_cols, "b_out_cols"), (halo_mask, "halo_mask"),
                (smask, "smask"), (xhT, "halo_x"),
            ):
                sLd.dma(eng.dma_start(out=dst[:], in_=I[src]))
            for lt in range(NLT):
                sLd.dma(eng.dma_start(out=xT[:, lt * HID:(lt + 1) * HID],
                                      in_=I["x"][lt * 128:(lt + 1) * 128, :]))
        SY(p0_sync)
        ld0 = sLd.n

        def p0_vec(eng):
            eng.memset(wpart[:, 3:4], 0.0)
            eng.memset(eps_col[:], EPS)
            eng.wait_ge(sLd.h, ld0)
            eng.tensor_scalar(mb_cols[:], b_cols[:, 32:67],
                              halo_mask[:, 0:1], None, ALU.mult)
        V(p0_vec)

        barrier()

        def p1_scalar(eng):
            for lt in range(NLT):
                xt = xT[:, lt * HID:(lt + 1) * HID]
                eng.activation(sp_scr[:], xt, AF.Identity, accum_out=ln_s[:, lt:lt + 1])
                eng.activation(sp_scr[:], xt, AF.Square, accum_out=ln_q[:, lt:lt + 1])
            eng.activation(sp_scr[0:3, :], xhT[:], AF.Identity, accum_out=hstat[:, 0:1])
            eng.activation(sp_scr[0:3, :], xhT[:], AF.Square, accum_out=hstat[:, 1:2])
            eng.activation(A_col[:], alog[:], AF.Exp)
        S(p1_scalar)
        c_p1s = sS.n

        def p1_vec(eng):
            eng.wait_ge(sS.h, c_p1s)
            eng.tensor_scalar(ln_m[:], ln_s[:], 1.0 / HID, None, ALU.mult)
            eng.tensor_scalar(ln_r[:], ln_q[:], 1.0 / HID, None, ALU.mult)
            eng.tensor_tensor(ln_s[:], ln_m[:], ln_m[:], ALU.mult)
            eng.tensor_sub(ln_r[:], ln_r[:], ln_s[:])
            eng.tensor_scalar(hstat[:, 2:3], hstat[:, 0:1], 1.0 / HID, None, ALU.mult)
            eng.tensor_scalar(hstat[:, 3:4], hstat[:, 1:2], 1.0 / HID, None, ALU.mult)
            eng.tensor_tensor(hstat[:, 4:5], hstat[:, 2:3], hstat[:, 2:3], ALU.mult)
            eng.tensor_sub(hstat[:, 3:4], hstat[:, 3:4], hstat[:, 4:5])
            eng.tensor_scalar(A_col[:], A_col[:], -1.0, None, ALU.mult)
            eng.tensor_scalar(dtbias2[:], dtb[:], 0.0, None, ALU.add)
        V(p1_vec)
        c_p1v = sV.n

        def p1_scalar2(eng):
            eng.wait_ge(sV.h, c_p1v)
            eng.activation(ln_r[:], ln_r[:], AF.Ln, bias=eps_col[:])
            eng.activation(ln_r[:], ln_r[:], AF.Exp, scale=-0.5)
            eng.activation(hstat[:, 3:4], hstat[:, 3:4], AF.Ln, bias=eps_col[0:3, :])
            eng.activation(hstat[:, 3:4], hstat[:, 3:4], AF.Exp, scale=-0.5)
        S(p1_scalar2)
        c_p1s2 = sS.n

        v_xn1_0 = sV.n
        def p1_vec2(eng):
            eng.wait_ge(sS.h, c_p1s2)
            for lt in range(NLT):
                eng.tensor_scalar(
                    xn1T[:, lt * HID:(lt + 1) * HID],
                    xT[:, lt * HID:(lt + 1) * HID],
                    ln_m[:, lt:lt + 1], ln_r[:, lt:lt + 1], ALU.subtract, ALU.mult)
            eng.tensor_scalar(xn1h[:], xhT[:], hstat[:, 2:3], hstat[:, 3:4],
                              ALU.subtract, ALU.mult)
        V(p1_vec2)

        t_p1_0 = sT.n
        s_p1e_0 = sS.n
        def p1_tensor(eng):
            for lt in range(NLT):
                eng.wait_ge(sV.h, v_xn1_0 + lt + 1)
                for half in range(2):
                    idx = lt * 2 + half
                    eng.fence()
                    if idx >= 2:
                        eng.wait_ge(sS.h, s_p1e_0 + idx - 1)
                    pv = pbig_b[:, half * 1024:(half + 1) * 1024]
                    for kk in range(8):
                        k = half * 8 + kk
                        eng.transpose(
                            pv[:, kk * 128:(kk + 1) * 128],
                            xn1T[:, lt * HID + k * 128:lt * HID + (k + 1) * 128],
                            identb[:])
            eng.wait_ge(sV.h, v_xn1_0 + NLT + 1)
            eng.wait_ge(sS.h, s_p1e_0 + 15)
            for k in range(16):
                eng.transpose(pE_b[:, k * 4:k * 4 + 3],
                              xn1h[:, k * 128:(k + 1) * 128], identb[0:3, 0:3])
        TE(p1_tensor)

        def p1_scalar3(eng):
            for lt in range(NLT):
                for half in range(2):
                    idx = lt * 2 + half
                    eng.wait_ge(sT.h, t_p1_0 + 8 * (idx + 1))
                    k0 = half * 8
                    eng.copy(
                        xn1F[:].rearrange("p (k t) -> p k t", k=16)
                        [:, k0:k0 + 8, lt * 128:(lt + 1) * 128],
                        pbig_b[:, half * 1024:(half + 1) * 1024]
                        .rearrange("p (k t) -> p k t", k=8))
            eng.wait_ge(sT.h, t_p1_0 + 128 + 16)
            eng.copy(
                xn1hF[:].rearrange("p (k t) -> p k t", k=16),
                pE_b[:, 0:64].rearrange("p (k t) -> p k t", k=16)[:, :, 0:3])
        S(p1_scalar3)

        barrier()
        dbg_store("xn1F", xn1F[:, :], [128, 16 * T], BF16)
        dbg_store("xn1hF", xn1hF[:], [128, 48], BF16)
        barrier()

        hbc_buf = [vf(OFF_B, 1027), vf(OFF_B + 1027, 1027)]
        cacc = [vf(OFF_B + 2054, 1024), vf(OFF_B + 3078, 1024)]
        hbs = [vb(OFF_B + 4102, 512), vb(OFF_B + 4614, 512)]
        wslab = [vb(OFF_B + 5126, 1024), vb(OFF_B + 6150, 1024)]

        T0 = sT.n; S0 = sS.n; V0 = sV.n
        CA0, CB0 = sCastA.n, sCastB.n
        mainbank = [(pA0, pA1), (pB0, pB1)]
        halobank = [pD, pE]

        def p2_castdma(eng):
            for d in range(NHB):
                if d >= 2:
                    eng.wait_ge(sT.h, T0 + 11 * (d - 2) + 3)
                sem = (sCastA, sCastB)[d % 2]
                sem.dma(eng.dma_start(
                    out=wslab[d % 2][:],
                    in_=I["w_hbc"][d].rearrange("i k j -> i (k j)")))
                if d % 2 == 0 and d // 2 <= 11:
                    i = d // 2
                    if i >= 2:
                        eng.wait_ge(sV.h, V0 + 5 * (2 * i - 2) + 5)
                    (sWs, sWs2)[i % 2].dma(eng.dma_start(
                        out=wsl[i % 2][:],
                        in_=I[("ws1", "ws2", "ws3")[i // 4]]
                        [:, (i % 4) * 2816:(i % 4 + 1) * 2816]))
        G(p2_castdma)

        def p2_tensor(eng):
            for d in range(NHB):
                eng.wait_ge((sCastA, sCastB)[d % 2].h,
                            (CA0, CB0)[d % 2] + (d // 2 + 1) * 16)
                if d >= 2:
                    eng.wait_ge(sS.h, S0 + 5 * (d - 2) + 3)
                mb = mainbank[d % 2]
                hb = halobank[d % 2]
                for k in range(16):
                    lhs = wslab[d % 2][:, k * 128:(k + 1) * 128]
                    eng.matmul(mb[0][:], lhs, xn1F[:, k * T:k * T + 512],
                               start=(k == 0), stop=(k == 15))
                    eng.matmul(mb[1][:], lhs, xn1F[:, k * T + 512:(k + 1) * T],
                               start=(k == 0), stop=(k == 15))
                    eng.matmul(hb[:, 0:3], lhs, xn1hF[:, k * 3:(k + 1) * 3],
                               start=(k == 0), stop=(k == 15))
                dd = d - 1
                eng.fence()
                if 0 <= dd <= 32:
                    eng.wait_ge(sS.h, S0 + 5 * dd + 4)
                    if dd >= 2:
                        eng.wait_ge(sS.h, S0 + 5 * (dd - 2) + 5)
                    pv = pbig_b[:, (dd % 2) * 1024:((dd % 2) + 1) * 1024]
                    src = hbs[dd % 2] if dd < 32 else B_F
                    for lt in range(8):
                        eng.transpose(pv[:, lt * 128:(lt + 1) * 128],
                                      src[:, lt * 128:(lt + 1) * 128], identb[:])
                else:
                    eng.pad(8)
        TE(p2_tensor)

        def p2_scalar(eng):
            for d in range(NHB):
                eng.wait_ge(sT.h, T0 + 11 * d + 3)
                if d >= 2 and d - 2 < 34:
                    eng.wait_ge(sV.h, V0 + 5 * (d - 1))
                mb = mainbank[d % 2]
                bias = b_cols[:, 32 + d:33 + d]
                if d < 34:
                    eng.activation(hbc_buf[d % 2][:, 3:515], mb[0][:], AF.Identity,
                                   bias=bias)
                    eng.activation(hbc_buf[d % 2][:, 515:1027], mb[1][:],
                                   AF.Identity, bias=bias)
                    eng.activation(hbc_buf[d % 2][:, 0:3], halobank[d % 2][:, 0:3],
                                   AF.Identity, bias=mb_cols[:, d:d + 1],
                                   scale=halo_mask[:, 0:1])
                else:
                    eng.activation(dtraw[:, 0:512], mb[0][0:64, :], AF.Identity,
                                   bias=bias[0:64, :])
                    eng.activation(dtraw[:, 512:1024], mb[1][0:64, :], AF.Identity,
                                   bias=bias[0:64, :])
                    eng.pad(1)
                if d < 34:
                    eng.wait_ge(sV.h, V0 + 5 * d + 4)
                    dst = (hbs[d % 2][:] if d < 32 else
                           (B_F[:] if d == 32 else C_F[:]))
                    eng.activation(dst, cacc[d % 2][:], AF.Silu)
                else:
                    eng.pad(1)
                if d <= 32:
                    eng.wait_ge(sT.h, T0 + 11 * (d + 1) + 11)
                    pv = pbig_b[:, (d % 2) * 1024:((d % 2) + 1) * 1024]
                    if d < 32:
                        eng.copy(
                            hsT[:, :].rearrange("p (l f) -> p l f", l=NLT)
                            [:, :, d * 128:(d + 1) * 128],
                            pv.rearrange("p (l f) -> p l f", l=8))
                    else:
                        eng.copy(
                            B_Tt[:].rearrange("p (l f) -> p l f", l=NCH),
                            pv.rearrange("p (l f) -> p l f", l=8))
                else:
                    eng.pad(1)
        S(p2_scalar)

        def p2_conv(eng):
            for d in range(34):
                eng.wait_ge(sS.h, S0 + 5 * d + 3)
                if d >= 2:
                    eng.wait_ge(sS.h, S0 + 5 * (d - 2) + 4)
                buf = hbc_buf[d % 2]; ca = cacc[d % 2]
                eng.tensor_scalar(ca[:], buf[:, 0:1024],
                                  convw[:, d * 4:d * 4 + 1], None, ALU.mult)
                for k in (1, 2, 3):
                    eng.scalar_tensor_tensor(ca[:], buf[:, k:k + 1024],
                                             convw[:, d * 4 + k:d * 4 + k + 1],
                                             ca[:], ALU.mult, ALU.add)
                if d % 2 == 0 and 2 <= d <= 24:
                    i = d // 2 - 1
                    eng.wait_ge((sWs, sWs2)[i % 2].h, 16 * (i // 2 + 1))
                    eng.tensor_reduce(wcol[:, i:i + 1], wsl[i % 2][:], AX,
                                      ALU.add, apply_absolute_value=True)
                else:
                    eng.pad(1)
            for i in range(3):
                eng.tensor_add(wpart[:, i:i + 1], wcol[:, 4 * i:4 * i + 1],
                               wcol[:, 4 * i + 1:4 * i + 2])
                eng.tensor_add(wpart[:, i:i + 1], wpart[:, i:i + 1],
                               wcol[:, 4 * i + 2:4 * i + 3])
                eng.tensor_add(wpart[:, i:i + 1], wpart[:, i:i + 1],
                               wcol[:, 4 * i + 3:4 * i + 4])
        V(p2_conv)

        barrier()
        dbg_store("hsT", hsT[:, 0:NLT * INTER], [128, NLT * INTER], BF16)
        dbg_store("B_F", B_F[:], [128, T], BF16)
        dbg_store("C_F", C_F[:], [128, T], BF16)
        dbg_store("B_Tt", B_Tt[:], [128, NCH * 128], BF16)
        dbg_store("dtraw", dtraw[:], [64, T])
        barrier()

        gslab = [vb(OFF_D, 2048), vb(OFF_D + 2048, 2048), vb(OFF_D + 4096, 2048)]
        bgr = vf(OFF_D + 6144, 512, rows=1)

        T0b = sT.n; S0b = sS.n; L0b = sLdB.n
        CR0 = [sCastA.n, sCastB.n, sCastC.n]

        def p2b_ld(eng):
            for n in range(8):
                if n >= 1:
                    eng.wait_ge(sT.h, T0b + 8 * n)
                sLdB.dma(eng.dma_start(
                    out=bgr,
                    in_=I["b_gate_row"][0:1, n * 512:(n + 1) * 512]))
        SY(p2b_ld)

        def p2b_castdma(eng):
            for n in range(8):
                for q in range(2):
                    si = 2 * n + q
                    if si >= 3:
                        n_old = (si - 3) // 2
                        eng.wait_ge(sT.h, T0b + 8 * (n_old + 1))
                    sem = (sCastA, sCastB, sCastC)[si % 3]
                    sem.dma(eng.dma_start(
                        out=gslab[si % 3][:],
                        in_=I["w_gate"][n].rearrange("i k j -> i (k j)")
                        [:, q * 8 * 512:(q + 1) * 8 * 512]))
        G(p2b_castdma)

        def p2b_tensor(eng):
            it = 0
            for n in range(8):
                eng.wait_ge(sLdB.h, L0b + (n + 1) * 16)
                for si in (2 * n, 2 * n + 1):
                    eng.wait_ge((sCastA, sCastB, sCastC)[si % 3].h,
                                CR0[si % 3] + (si // 3 + 1) * 16)
                g0 = gslab[(2 * n) % 3]
                g1 = gslab[(2 * n + 1) % 3]
                for t in range(8):
                    bank = (pA0, pA1, pB0, pB1)[it % 4]
                    eng.fence()
                    if it >= 4:
                        eng.wait_ge(sS.h, S0b + it - 3)
                    for k in range(16):
                        gs = g0 if k < 8 else g1
                        eng.matmul(bank[:],
                                   xn1F[:, k * T + t * 128:k * T + (t + 1) * 128],
                                   gs[:, (k % 8) * 512:(k % 8 + 1) * 512],
                                   start=(k == 0), stop=False)
                    eng.matmul(bank[:], ones_row[:], bgr,
                               start=False, stop=True)
                    it += 1
        TE(p2b_tensor)

        def p2b_scalar(eng):
            it = 0
            for n in range(8):
                for t in range(8):
                    bank = (pA0, pA1, pB0, pB1)[it % 4]
                    eng.wait_ge(sT.h, T0b + it + 1)
                    eng.activation(
                        sgT[:, t * INTER + n * 512:t * INTER + (n + 1) * 512],
                        bank[:], AF.Silu)
                    it += 1
        S(p2b_scalar)

        barrier()
        dbg_store("sgT", sgT[:, 0:NLT * INTER], [128, NLT * INTER], BF16)
        barrier()

        S_in = vb(0, 2048)
        dtT = vb(4096, 256)
        dtdT = vb(4352, 256)
        sdT = vb(4608, 256)
        dkE = vf(4864, 512)
        acum = vf(5376, 1024, rows=64)
        dt_sp = vf(6400, 1024, rows=64)
        abt_scr = vf(7424, 128, rows=64)
        acumT = vf(7552, 512)
        dmj_scr = vf(8064, 64)
        omask = vf(8128, 7)
        wsum = vf(8135, 3)
        pE_scr3 = vf(8138, 1, rows=3)
        OFF_E = 47616
        AFLAT = vf(OFF_E, 1024, rows=1)
        sp4 = vf(OFF_E, 1024, rows=64)
        sdF_scr = vf(OFF_D, 128, rows=64)
        dtd_scr = vf(OFF_D + 128, 128, rows=64)
        M_grp = vb(6400, 512)
        Gm = vb(6912, 64)
        scrY7 = vf(6976, 512)
        hsdd = vb(OFF_D, 2048)
        hs_dt = vb(OFF_D + 2048, 2048)
        xg = vb(OFF_D + 4096, 2048)
        hsD_scr = vb(OFF_D + 6144, 256)
        hsDr = [hsD_scr, vb(OFF_D + 6400, 256)]
        scrY = vf(OFF_D + 2048, 512)

        def p4_scalar(eng):
            eng.activation(sp4[:], dtraw[:], AF.Abs, bias=dtbias2[:])
            eng.activation(sp4[:], sp4[:], AF.Exp, scale=-1.0)
            eng.activation(sp4[:], sp4[:], AF.Ln, bias=ones_col[0:64, :])
        S(p4_scalar)
        c4s = sS.n

        def p4_vec(eng):
            eng.tensor_scalar(dt_sp[:], dtraw[:], dtbias2[:], 0.0, ALU.add, ALU.max)
            eng.wait_ge(sS.h, c4s)
            eng.tensor_add(dt_sp[:], dt_sp[:], sp4[:])
            for ch in range(NCH):
                eng.tensor_scalar(abt_scr[:], dt_sp[:, ch * 128:(ch + 1) * 128],
                                  A_col[:], None, ALU.mult)
                eng.tensor_tensor_scan(acum[:, ch * 128:(ch + 1) * 128],
                                       abt_scr[:], abt_scr[:], 0.0,
                                       ALU.add, ALU.bypass)
            eng.memset(akrow[0:1, 576:640], 0.0)
        V(p4_vec)
        c4v = sV.n

        t4_0 = sT.n
        s4_0 = sS.n
        v4_0 = sV.n

        def p4b_vec(eng):
            for ch in range(NCH):
                eng.wait_ge(sS.h, s4_0 + 8 * ch + 1)
                eng.tensor_mul(dtd_scr[:], dt_sp[:, ch * 128:(ch + 1) * 128],
                               dtd_scr[:])
        V(p4b_vec)

        def p4b_tensor(eng):
            for ch in range(NCH):
                eng.fence()
                eng.wait_ge(sV.h, v4_0 + ch + 1)
                eng.wait_ge(sS.h, s4_0 + 8 * ch + 2)
                if ch >= 1:
                    eng.wait_ge(sS.h, s4_0 + 8 * (ch - 1) + 7)
                eng.transpose(pE[:, 0:64], dt_sp[:, ch * 128:(ch + 1) * 128],
                              identf[0:64, 0:64])
                eng.transpose(pE[:, 64:128], dtd_scr[:], identf[0:64, 0:64])
                eng.transpose(pE[:, 128:192], sdF_scr[:], identf[0:64, 0:64])
                eng.transpose(pE[0:1, 192:256],
                              acum[:, ch * 128 + 127:ch * 128 + 128],
                              identf[0:64, 0:64])
                eng.transpose(pD[:, 64:128], acum[:, ch * 128:(ch + 1) * 128],
                              identf[0:64, 0:64])
                eng.fence()
                eng.wait_ge(sS.h, s4_0 + 8 * ch + 7)
                if ch >= 1:
                    eng.wait_ge(sS.h, s4_0 + 8 * (ch - 1) + 8)
                eng.matmul(pD[:, 0:64], ones_row[:],
                           akrow[0:1, ch * 64:(ch + 1) * 64], start=True, stop=True)
        TE(p4b_tensor)

        def p4b_scalar(eng):
            for ch in range(NCH):
                akcol = acum[:, ch * 128 + 127:ch * 128 + 128]
                eng.wait_ge(sV.h, c4v)
                eng.activation(dtd_scr[:], acum[:, ch * 128:(ch + 1) * 128],
                               AF.Exp, bias=akcol, scale=-1.0)
                eng.activation(sdF_scr[:], acum[:, ch * 128:(ch + 1) * 128], AF.Exp)
                eng.wait_ge(sT.h, t4_0 + 6 * ch + 5)
                eng.copy(dtT[:, ch * 64:(ch + 1) * 64], pE[:, 0:64])
                eng.copy(dtdT[:, ch * 64:(ch + 1) * 64], pE[:, 64:128])
                eng.copy(sdT[:, ch * 64:(ch + 1) * 64], pE[:, 128:192])
                eng.copy(akrow[0:1, ch * 64:(ch + 1) * 64], pE[0:1, 192:256])
                eng.copy(acumT[:, ch * 64:(ch + 1) * 64], pD[:, 64:128])
                eng.wait_ge(sT.h, t4_0 + 6 * ch + 6)
                eng.activation(dkE[:, ch * 64:(ch + 1) * 64], pD[:, 0:64], AF.Exp)
        S(p4b_scalar)
        c4b_s = sS.n

        def p4c_vec(eng):
            eng.wait_ge(sS.h, c4b_s)
            eng.tensor_reduce(
                akrow[0:1, 512:576],
                akrow[0:1, 0:512].rearrange("p (c h) -> p h c", c=NCH), AX, ALU.add)
        V(p4c_vec)
        c4c_v = sV.n

        def p4c_tensor(eng):
            eng.fence()
            eng.wait_ge(sV.h, c4c_v)
            eng.wait_ge(sS.h, c4b_s)
            eng.matmul(pD[:, 64:128], ones_row[:], akrow[0:1, 512:576],
                       start=True, stop=True)
            eng.matmul(pD[:, 128:192], ones_row[:], d_rowt[:], start=True, stop=True)
        TE(p4c_tensor)
        c4c_t = sT.n

        def p4c_scalar(eng):
            eng.wait_ge(sT.h, c4c_t)
            eng.activation(dtil[:], pD[:, 64:128], AF.Exp)
            eng.copy(Dtile[:], pD[:, 128:192])
        S(p4c_scalar)

        barrier()
        dbg_store("acum", acum[:], [64, T])
        dbg_store("dt_sp", dt_sp[:], [64, T])
        dbg_store("dtT", dtT[:, :], [128, 512], BF16)
        dbg_store("dtdT", dtdT[:, :], [128, 512], BF16)
        dbg_store("sdT", sdT[:, :], [128, 512], BF16)
        dbg_store("dkE", dkE[:], [128, 512])
        dbg_store("dtil", dtil[:], [128, 64])
        barrier()

        t5_0 = sT.n
        v5_0 = sV.n

        def p5_vec(eng):
            eng.memset(S_in[:], 0.0)
            for ch in range(NCH):
                if ch >= 1:
                    eng.wait_ge(sT.h, t5_0 + 8 * ch)
                eng.tensor_tensor(
                    hsdd[:].rearrange("p (h q) -> p h q", h=64),
                    hsT[:, ch * 4096:(ch + 1) * 4096]
                    .rearrange("p (h q) -> p h q", h=64),
                    dtdT[:, ch * 64:(ch + 1) * 64][:, :, None]
                    .broadcast_to([128, 64, 64]),
                    ALU.mult)
                for g in range(8):
                    eng.wait_ge(sT.h, t5_0 + 8 * ch + g + 1)
                    eng.tensor_tensor(
                        scrY[:].rearrange("p (h q) -> p h q", h=8),
                        S_in[:, g * 512:(g + 1) * 512]
                        .rearrange("p (h q) -> p h q", h=8),
                        dkE[:, ch * 64 + g * 8:ch * 64 + (g + 1) * 8][:, :, None]
                        .broadcast_to([128, 8, 64]),
                        ALU.mult)
                    eng.tensor_add(S_in[:, g * 512:(g + 1) * 512], scrY[:],
                                   (pD if g % 2 == 0 else pE)[:])
        def p5_tensor(eng):
            for ch in range(NCH):
                eng.fence()
                eng.wait_ge(sV.h, v5_0 + 1 + ch * 17 + 1)
                for g in range(8):
                    bank = pD if g % 2 == 0 else pE
                    if g >= 2:
                        eng.wait_ge(sV.h, v5_0 + 1 + ch * 17 + 1 + 2 * (g - 1))
                    eng.matmul(bank[:],
                               B_Tt[:, ch * 128:(ch + 1) * 128],
                               hsdd[:, g * 512:(g + 1) * 512],
                               start=True, stop=True)
                    eng.fence()
        TE(p5_tensor)
        V(p5_vec)

        barrier()
        dbg_store("O_st", S_in[:], [128, 4096], BF16)
        barrier()

        def p5c_sync(eng):
            sLd.dma(eng.dma_start(out=cc_in[:, 4096:4160], in_=dtil[:]))
            sLd.dma(eng.dma_start(out=cc_in[:, 4160:4164], in_=wpart[:]))
        SY(p5c_sync)
        c5_ld = sLd.n

        def p5c_gp(eng):
            sCastC.dma(eng.dma_start(out=cc_in[:, 0:4096], in_=S_in[:]))
            eng.wait_ge(sCastC.h, sCastC.n)
            eng.wait_ge(sLd.h, c5_ld)
            eng.collective_compute(
                "AllGather", ALU.bypass, [list(range(NCORE))],
                [cc_in[:]], [cc_out[:]]).then_inc(sCC.h, 1)
        block.gpsimd(p5c_gp)
        sCC.n += 1
        c6_castc = sCastC.n

        rb = vf(OFF_D, 4164)
        qthr = sb("qthr", [128, 10])

        v6_rd = {}
        LA0, LB0 = sLdA.n, sLdB.n

        def p6_vec(eng):
            eng.tensor_scalar(omask[:], smask[:], -1.0, 1.0, ALU.mult, ALU.add)
            eng.memset(wsum[:], 0.0)
            eng.wait_ge(sCastC.h, c6_castc)
            eng.memset(S_in[:], 0.0)
            for j in range(NCORE):
                eng.wait_ge((sLdA, sLdB)[j % 2].h,
                            (LA0, LB0)[j % 2] + (j // 2 + 1) * 16)
                eng.tensor_add(wsum[:], wsum[:], rb[:, 4160:4163])
                if j < 7:
                    eng.scalar_tensor_tensor(
                        dmj_scr[:], rb[:, 4096:4160], smask[:, j:j + 1],
                        omask[:, j:j + 1].broadcast_to([128, 64]),
                        ALU.mult, ALU.add)
                    eng.tensor_tensor(
                        S_in[:].rearrange("p (h q) -> p h q", h=64),
                        S_in[:].rearrange("p (h q) -> p h q", h=64),
                        dmj_scr[:][:, :, None].broadcast_to([128, 64, 64]),
                        ALU.mult)
                    eng.scalar_tensor_tensor(
                        S_in[:], rb[:, 0:4096], smask[:, j:j + 1], S_in[:],
                        ALU.mult, ALU.add)
                v6_rd[j] = sV.n
        V(p6_vec)

        def p6_sync(eng):
            eng.wait_ge(sCC.h, sCC.n)
            for j in range(NCORE):
                if j >= 1:
                    eng.wait_ge(sV.h, v6_rd[j - 1])
                sem = (sLdA, sLdB)[j % 2]
                sem.dma(eng.dma_start(out=rb[:],
                                      in_=cc_out[j * 128:(j + 1) * 128, :]))
        SY(p6_sync)
        c6_v = sV.n

        def p6_tensor(eng):
            eng.fence()
            eng.wait_ge(sV.h, c6_v)
            eng.matmul(pE[0:3, 0:1], wsum[:, 0:3], ones_col[:], start=True, stop=True)
        TE(p6_tensor)
        c6_t = sT.n

        def p6_scalar_a(eng):
            eng.wait_ge(sT.h, c6_t)
            eng.copy(pE_scr3[:], pE[0:3, 0:1])
        S(p6_scalar_a)
        c6_s = sS.n

        def p6_tensor_b(eng):
            eng.fence()
            eng.wait_ge(sS.h, c6_s)
            eng.transpose(pD[0:1, 0:3], pE_scr3[:], identf[0:3, 0:3])
        TE(p6_tensor_b)
        c6_t2 = sT.n

        def p6_scalar_b(eng):
            eng.wait_ge(sT.h, c6_t2)
            eng.copy(akrow[0:1, 576:579], pD[0:1, 0:3])
        S(p6_scalar_b)
        c6_s2 = sS.n

        def p6_tensor_c(eng):
            eng.fence()
            eng.wait_ge(sS.h, c6_s2)
            eng.matmul(pD[:, 192:256], ones_row[:],
                       akrow[0:1, 576:640], start=True, stop=True)
        TE(p6_tensor_c)
        c6_t3 = sT.n

        def p6_vec_b(eng):
            eng.wait_ge(sT.h, c6_t3)
            eng.tensor_scalar(wk[:, 0:3], pD[:, 192:195], 1.0 / WCNT, None, ALU.mult)
            eng.tensor_scalar(qthr[:, 0:3], wk[:, 0:3], 0.5, None, ALU.mult)
            eng.tensor_scalar(qthr[:, 3:6], qthr[:, 0:3], -1.0, None, ALU.mult)
            eng.tensor_scalar(qthr[:, 6:7], wk[:, 0:1], 1.0 / 127.0, None, ALU.mult)
            eng.tensor_scalar(qthr[:, 7:8], wk[:, 2:3], 0.5 / 127.0, None, ALU.mult)
            eng.tensor_scalar(qthr[:, 8:9], wk[:, 1:2], 0.5 / 127.0, None, ALU.mult)
        V(p6_vec_b)

        barrier()
        dbg_store("S0", S_in[:], [128, 4096], BF16)
        dbg_store("wk", wk[:, 0:3], [128, 3])
        barrier()

        rms_ms = sb("rms_ms", [128, NCH]); rms_rs = sb("rms_rs", [128, NCH])
        t7_0 = sT.n; v7_0 = sV.n; s7_0 = sS.n
        L7A = sLdA.n

        def te_L_pos(ch, g):
            return t7_0 + 41 * ch + 9 + (2 if g == 0 else 4 * g)

        def te_Yd_pos(ch, g):
            return t7_0 + 41 * ch + 9 + (4 * g + 6 if g < 7 else 32)

        def te_Yoff_pos(ch, g):
            return t7_0 + 41 * ch + 9 + (4 * g + 5 if g < 7 else 31)

        def v_pos(ch, k):
            return v7_0 + 63 * ch + k

        def s_pos(ch, k):
            return s7_0 + 19 * ch + k

        g7_0 = sG.n

        def g_pos(ch, k):
            return g7_0 + 10 * ch + k

        def p7_sync(eng):
            for ch in range(NCH):
                for g in range(8):
                    idx = ch * 8 + g
                    if idx >= 1:
                        pch, pg = divmod(idx - 1, 8)
                        eng.wait_ge(sT.h, te_L_pos(pch, pg))
                    sLdA.dma(eng.dma_start(
                        out=AFLAT,
                        in_=acum[g * 8:(g + 1) * 8, ch * 128:(ch + 1) * 128]))
        SY(p7_sync)

        def p7_gp(eng):
            for ch in range(NCH):
                if ch >= 1:
                    eng.wait_ge(sT.h, t7_0 + 41 * ch)
                    eng.tensor_tensor(
                        hsdd[:].rearrange("p (h q) -> p h q", h=64),
                        hsT[:, (ch - 1) * 4096:ch * 4096]
                        .rearrange("p (h q) -> p h q", h=64),
                        dtdT[:, (ch - 1) * 64:ch * 64][:, :, None]
                        .broadcast_to([128, 64, 64]), ALU.mult)
                else:
                    eng.pad(1)
                if ch >= 1:
                    eng.wait_ge(sS.h, s_pos(ch - 1, 19))
                eng.tensor_tensor(
                    hs_dt[:].rearrange("p (h q) -> p h q", h=64),
                    hsT[:, ch * 4096:(ch + 1) * 4096]
                    .rearrange("p (h q) -> p h q", h=64),
                    dtT[:, ch * 64:(ch + 1) * 64][:, :, None]
                    .broadcast_to([128, 64, 64]), ALU.mult)
                for g in range(8):
                    if g >= 2:
                        eng.wait_ge(sT.h, te_Yd_pos(ch, g - 2))
                    elif ch >= 1:
                        eng.wait_ge(sT.h, te_Yd_pos(ch - 1, 6 + g))
                    eng.tensor_tensor(
                        hsDr[g % 2][:].rearrange("p (h q) -> p h q", h=8),
                        hsT[:, ch * 4096 + g * 512:ch * 4096 + (g + 1) * 512]
                        .rearrange("p (h q) -> p h q", h=8),
                        Dtile[:, g * 8:(g + 1) * 8][:, :, None]
                        .broadcast_to([128, 8, 64]), ALU.mult)
        G(p7_gp)

        def p7_tensor(eng):
            for ch in range(NCH):
                if ch >= 1:
                    for g in range(8):
                        eng.fence()
                        eng.wait_ge(sG.h, g_pos(ch, 1))
                        if g >= 2:
                            eng.wait_ge(sV.h, v_pos(ch, 1 + 2 * (g - 1)))
                        bank = pD if g % 2 == 0 else pE
                        eng.matmul(bank[:],
                                   B_Tt[:, (ch - 1) * 128:ch * 128],
                                   hsdd[:, g * 512:(g + 1) * 512],
                                   start=True, stop=True)
                else:
                    eng.pad(8)
                eng.fence()
                if ch >= 1:
                    eng.wait_ge(sV.h, v_pos(ch - 1, 59))
                eng.matmul(pbig[:, 0:128], B_F[:, ch * 128:(ch + 1) * 128],
                           C_F[:, ch * 128:(ch + 1) * 128], start=True, stop=True)
                def ysec(j, need_m=False):
                    eng.fence()
                    if need_m:
                        eng.wait_ge(sV.h, v_pos(ch, 19 + 5 * j + 5))
                    if j == 0:
                        eng.wait_ge(sG.h, g_pos(ch, 2))
                    ydb = pA0 if j % 2 == 0 else pB0
                    yob = pA1 if j % 2 == 0 else pB1
                    if j >= 2:
                        eng.wait_ge(sV.h, v_pos(ch, 19 + 5 * (j - 1) + 3))
                    elif ch >= 1:
                        eng.wait_ge(sV.h, v_pos(ch - 1, (57, 61)[j]))
                    eng.wait_ge(sV.h, v_pos(ch, 2 * j + 3))
                    eng.matmul(yob[:], C_F[:, ch * 128:(ch + 1) * 128],
                               S_in[:, j * 512:(j + 1) * 512],
                               start=True, stop=True)
                    for jj in range(8):
                        h = j * 8 + jj
                        eng.matmul(ydb[:, jj * 64:(jj + 1) * 64],
                                   M_grp[:, jj * 128:(jj + 1) * 128],
                                   hs_dt[:, h * 64:(h + 1) * 64],
                                   start=(jj == 0), stop=False)
                    eng.wait_ge(sG.h, g_pos(ch, 3 + j))
                    eng.matmul(ydb[:], identb[:], hsDr[j % 2][:],
                               start=False, stop=True)

                for g in range(8):
                    idx = ch * 8 + g
                    eng.fence()
                    eng.wait_ge(sLdA.h, L7A + (idx + 1) * 16)
                    if g >= 1:
                        eng.wait_ge(sV.h, v_pos(ch, 19 + 5 * (g - 1) + 5))
                    else:
                        if ch >= 1:
                            eng.wait_ge(sV.h, v_pos(ch - 1, 59))
                        eng.wait_ge(sV.h, v_pos(ch, 19))
                    for hh in range(2):
                        eng.matmul(pbig[:, hh * 512:(hh + 1) * 512], ones_row[:],
                                   AFLAT[0:1, hh * 512:(hh + 1) * 512],
                                   start=True, stop=True)
                    if g >= 1:
                        ysec(g - 1)
                ysec(7, need_m=True)
        TE(p7_tensor)

        def p7_vec(eng):
            for ch in range(NCH):
                eng.pad(1)
                if ch >= 1:
                    for g in range(8):
                        eng.wait_ge(sT.h, t7_0 + 41 * ch + g + 1)
                        bank = pD if g % 2 == 0 else pE
                        eng.tensor_tensor(
                            scrY7[:].rearrange("p (h q) -> p h q", h=8),
                            S_in[:, g * 512:(g + 1) * 512]
                            .rearrange("p (h q) -> p h q", h=8),
                            dkE[:, (ch - 1) * 64 + g * 8:(ch - 1) * 64 + (g + 1) * 8]
                            [:, :, None].broadcast_to([128, 8, 64]), ALU.mult)
                        eng.tensor_add(S_in[:, g * 512:(g + 1) * 512], scrY7[:],
                                       bank[:])
                else:
                    eng.pad(16)
                eng.pad(1)
                eng.wait_ge(sT.h, t7_0 + 41 * ch + 9)
                eng.tensor_tensor(Gm[:], pbig[:, 0:128], trimask[:], ALU.mult)

                def ychain(g):
                    ydb = pA0 if g % 2 == 0 else pB0
                    yob = pA1 if g % 2 == 0 else pB1
                    eng.wait_ge(sT.h, te_Yoff_pos(ch, g))
                    eng.tensor_tensor(
                        scrY7[:].rearrange("p (h q) -> p h q", h=8),
                        yob[:].rearrange("p (h q) -> p h q", h=8),
                        sdT[:, ch * 64 + g * 8:ch * 64 + (g + 1) * 8][:, :, None]
                        .broadcast_to([128, 8, 64]), ALU.mult)
                    eng.wait_ge(sT.h, te_Yd_pos(ch, g))
                    eng.tensor_add(scrY7[:], scrY7[:], ydb[:])
                    eng.tensor_tensor(
                        xg[:, g * 512:(g + 1) * 512], scrY7[:],
                        sgT[:, ch * 4096 + g * 512:ch * 4096 + (g + 1) * 512],
                        ALU.mult)

                for g in range(8):
                    eng.wait_ge(sT.h, te_L_pos(ch, g))
                    eng.tensor_tensor(
                        pbig[:].rearrange("p (h q) -> p h q", h=8),
                        pbig[:].rearrange("p (h q) -> p h q", h=8),
                        acumT[:, ch * 64 + g * 8:ch * 64 + (g + 1) * 8][:, :, None]
                        .broadcast_to([128, 8, 128]), ALU.subtract)
                    if g >= 1:
                        ychain(g - 1)
                    else:
                        eng.pad(3)
                    eng.wait_ge(sS.h, s_pos(ch, 2 * g + 2))
                    eng.tensor_tensor(
                        M_grp[:].rearrange("p (h q) -> p h q", h=8),
                        pbig[:].rearrange("p (h q) -> p h q", h=8),
                        Gm[:][:, None, :].broadcast_to([128, 8, 128]), ALU.mult)
                ychain(7)
                eng.wait_ge(sS.h, s_pos(ch, 19))
                eng.tensor_scalar(sgT[:, ch * 4096:(ch + 1) * 4096], xg[:],
                                  rms_rs[:, ch:ch + 1], None, ALU.mult)
        V(p7_vec)

        def p7_scalar(eng):
            for ch in range(NCH):
                for g in range(8):
                    eng.wait_ge(sV.h, v_pos(ch, 19 + 5 * g + 1))
                    eng.activation(pbig[:], pbig[:], AF.Relu, scale=-1.0)
                    eng.activation(pbig[:], pbig[:], AF.Exp, scale=-1.0)
                eng.wait_ge(sV.h, v_pos(ch, 62))
                eng.wait_ge(sT.h, t7_0 + 41 * ch + 41)
                eng.activation(hs_dt[:], xg[:], AF.Square,
                               accum_out=rms_ms[:, ch:ch + 1])
                eng.activation(rms_rs[:, ch:ch + 1], rms_ms[:, ch:ch + 1],
                               AF.Ln, bias=eps_col[:], scale=1.0 / INTER)
                eng.activation(rms_rs[:, ch:ch + 1], rms_rs[:, ch:ch + 1],
                               AF.Exp, scale=-0.5)
        S(p7_scalar)

        barrier()
        dbg_store("xgn", sgT[:, 0:NLT * INTER], [128, NLT * INTER], BF16)
        barrier()

        t8_0 = sT.n; s8_0 = sS.n
        def p8_tensor(eng):
            for ch in range(NCH):
                for b in range(4):
                    idx = ch * 4 + b
                    eng.fence()
                    if idx >= 2:
                        eng.wait_ge(sS.h, s8_0 + idx - 1)
                    pv = pbig_b[:, (idx % 2) * 1024:((idx % 2) + 1) * 1024]
                    for jj in range(8):
                        j = b * 8 + jj
                        eng.transpose(
                            pv[:, jj * 128:(jj + 1) * 128],
                            sgT[:, ch * 4096 + j * 128:ch * 4096 + (j + 1) * 128],
                            identb[:])
        TE(p8_tensor)

        def p8_scalar(eng):
            for ch in range(NCH):
                for b in range(4):
                    idx = ch * 4 + b
                    eng.wait_ge(sT.h, t8_0 + 8 * (idx + 1))
                    eng.copy(
                        xgnF[:].rearrange("p (k t) -> p k t", k=32)
                        [:, b * 8:(b + 1) * 8, ch * 128:(ch + 1) * 128],
                        pbig_b[:, (idx % 2) * 1024:((idx % 2) + 1) * 1024]
                        .rearrange("p (k t) -> p k t", k=8))
        S(p8_scalar)

        barrier()

        wof = [vf(0, 2048), vf(2048, 2048)]
        wob = [vb(4096, 1024), vb(5120, 1024)]
        attn_scr = [vf(OFF_D, 1024), vf(OFF_D + 1024, 1024)]

        t9_0 = sT.n; v9_0 = sV.n; s9_0 = sS.n; g9_0 = sG.n
        L9A, L9B = sLdA.n, sLdB.n

        def p9_sync(eng):
            for lt in range(NLT):
                sLd.dma(eng.dma_start(out=x2[:, lt * HID:(lt + 1) * HID],
                                      in_=I["x"][lt * 128:(lt + 1) * 128, :]))
            for d in range(16):
                for hh in range(2):
                    idx = d * 2 + hh
                    if idx >= 2:
                        if hh == 0:
                            eng.wait_ge(sV.h, v9_0 + 2 * (d - 1) + 1)
                        else:
                            eng.wait_ge(sG.h, g9_0 + d)
                    sem = (sLdA, sLdB)[idx % 2]
                    sem.dma(eng.dma_start(
                        out=wof[idx % 2][:],
                        in_=I["w_out"][d].rearrange("i k j -> i (k j)")
                        [:, hh * 2048:(hh + 1) * 2048]))
        SY(p9_sync)
        c9_ld = sLd.n

        def p9_gp(eng):
            for d in range(16):
                idx = d * 2 + 1
                eng.wait_ge((sLdA, sLdB)[idx % 2].h,
                            (L9A, L9B)[idx % 2] + (idx // 2 + 1) * 16)
                if d >= 1:
                    eng.wait_ge(sT.h, t9_0 + 10 * (d - 1) + 2)
                eng.tensor_tensor(
                    wob[idx % 2][:].rearrange("p (k j) -> p k j", k=16),
                    wof[idx % 2][:].rearrange("p (k j) -> p k j", k=16),
                    gam[:, 16:32][:, :, None]
                    .broadcast_to([128, 16, 128]), ALU.mult)
        G(p9_gp)

        def p9_vec(eng):
            for d in range(16):
                idx = d * 2
                eng.wait_ge((sLdA, sLdB)[idx % 2].h,
                            (L9A, L9B)[idx % 2] + (idx // 2 + 1) * 16)
                if d >= 1:
                    eng.wait_ge(sT.h, t9_0 + 10 * (d - 1) + 1)
                eng.tensor_tensor(
                    wob[idx % 2][:].rearrange("p (k j) -> p k j", k=16),
                    wof[idx % 2][:].rearrange("p (k j) -> p k j", k=16),
                    gam[:, 0:16][:, :, None]
                    .broadcast_to([128, 16, 128]), ALU.mult)
                eng.wait_ge(sT.h, t9_0 + 10 * d + 10)
                eng.wait_ge(sLd.h, c9_ld)
                eng.tensor_add(
                    x2[:].rearrange("p (l f) -> p l f", l=NLT)
                    [:, :, d * 128:(d + 1) * 128],
                    x2[:].rearrange("p (l f) -> p l f", l=NLT)
                    [:, :, d * 128:(d + 1) * 128],
                    pbig[:].rearrange("p (l f) -> p l f", l=8))
        V(p9_vec)

        def p9_tensor_real(eng):
            nt = 0
            for d in range(16):
                bankT = (pA0, pA1) if d % 2 == 0 else (pB0, pB1)
                for hh in range(2):
                    idx = d * 2 + hh
                    eng.fence()
                    if hh == 0:
                        eng.wait_ge(sV.h, v9_0 + 2 * d + 1)
                    else:
                        eng.wait_ge(sG.h, g9_0 + d + 1)
                    if d >= 2:
                        eng.wait_ge(sS.h, s9_0 + 2 * (d - 2) + 2)
                    for kk in range(16):
                        k = hh * 16 + kk
                        for th in range(2):
                            eng.matmul(bankT[th][:],
                                       wob[idx % 2][:, kk * 128:(kk + 1) * 128],
                                       xgnF[:, k * 1024 + th * 512:
                                            k * 1024 + (th + 1) * 512],
                                       start=(k == 0), stop=(k == 31))
                eng.fence()
                eng.wait_ge(sS.h, s9_0 + 2 * d + 2)
                if d >= 2:
                    eng.wait_ge(sV.h, v9_0 + 2 * (d - 2) + 2)
                for lt in range(8):
                    eng.transpose(pbig[:, lt * 128:(lt + 1) * 128],
                                  attn_scr[d % 2][:, lt * 128:(lt + 1) * 128],
                                  identf[:])
        TE(p9_tensor_real)

        def p9_scalar(eng):
            for d in range(16):
                bankT = (pA0, pA1) if d % 2 == 0 else (pB0, pB1)
                eng.wait_ge(sT.h, t9_0 + 10 * d + 2)
                for th in range(2):
                    eng.activation(attn_scr[d % 2][:, th * 512:(th + 1) * 512],
                                   bankT[th][:], AF.Identity,
                                   bias=b_out_cols[:, d:d + 1])
        S(p9_scalar)

        barrier()
        dbg_store("x2", x2[:, 0:NLT * HID], [128, NLT * HID])
        barrier()

        fq1 = sb("fq1", [128, 8]); fq3 = sb("fq3", [128, 8])
        s10 = vf(OFF_D, 2048)
        xqT = vb(OFF_D + 2048, 4096)
        scrF = vf(OFF_D, 512)
        scrF2 = vf(OFF_D + 512, 512)
        qv_scr = vf(OFF_D + 1024, 2048)
        xqF = vb(OFF_A, 4096)
        hbuf = vb(OFF_A + 4096, 11264)
        amax_acc = vf(OFF_A + 15360, 512)
        amin_acc = vf(OFF_A + 15872, 512)
        SCLH = vf(0, 512)
        RS1 = vf(0, 512); RS2 = vf(512, 512)
        rowbuf = vf(1024, 512, rows=1)
        t1bR = [vb(1536, 1024), vb(2560, 1024)]
        t3bR = [vb(3584, 1024), vb(4608, 1024)]
        sg4 = [vb(5632 + i * 512, 512) for i in range(4)]
        wring4 = [vf(OFF_D + 1024 + i * 1024, 1024) for i in range(4)]
        w2tR = [vb(1536, 2816), vb(4352, 2816)]
        w2r2 = [vf(OFF_D + 1024 + j * 2048, 2048) for j in range(2)]
        sgnw = [vb(OFF_D + 5120 + i * 1024, 1024) for i in range(2)]
        ffn_scr = vf(OFF_D, 512)

        def build_bcast(cols, dst):
            vpre = sV.n; spre = sS.n
            def _t1(eng, cols=cols, vpre=vpre, spre=spre):
                eng.fence()
                eng.wait_ge(sV.h, vpre)
                eng.wait_ge(sS.h, spre)
                for i, cap in enumerate(cols):
                    eng.transpose(pE[0:1, i * 128:(i + 1) * 128], cap, identf[:])
                eng.fence()
            TE(_t1)
            tpre = sT.n
            def _s1(eng, tpre=tpre):
                eng.wait_ge(sT.h, tpre)
                eng.copy(rowbuf, pE[0:1, 0:512])
            S(_s1)
            spost = sS.n; vpre2 = sV.n
            def _t2(eng, spost=spost, vpre2=vpre2):
                eng.fence()
                eng.wait_ge(sS.h, spost)
                eng.wait_ge(sV.h, vpre2)
                eng.matmul(pD[:], ones_row[:], rowbuf, start=True, stop=True)
                eng.fence()
            TE(_t2)
            tpost = sT.n
            def _v1(eng, dst=dst, tpost=tpost):
                eng.wait_ge(sT.h, tpost)
                eng.tensor_copy(dst, pD[:])
            V(_v1)

        for HF in range(2):
            lts = list(range(HF * 4, HF * 4 + 4))
            v10_0 = sV.n; s10_0 = sS.n

            stq = sSt.n

            def lnexp_cnt(i, s0=s10_0):
                return s0 + 4 * (i + 1) + 2 if i < 3 else s0 + 18

            def p10_scalar(eng, lts=lts, v10_0=v10_0, stq=stq):
                eng.wait_ge(sSt.h, stq)
                for i, lt in enumerate(lts):
                    x2t = x2[:, lt * HID:(lt + 1) * HID]
                    if i >= 1:
                        pl = lts[i - 1]
                        eng.wait_ge(sV.h, v10_0 + 15 * (i - 1) + 4)
                        eng.activation(q_r[:, pl:pl + 1], q_r[:, pl:pl + 1],
                                       AF.Ln, bias=eps_col[:])
                        eng.activation(q_r[:, pl:pl + 1], q_r[:, pl:pl + 1],
                                       AF.Exp, scale=-0.5)
                    else:
                        eng.pad(2)
                    eng.wait_ge(sV.h, v10_0 + 15 * (i - 1) + 13)
                    eng.activation(s10[:], x2t, AF.Identity,
                                   accum_out=rms_ms[:, lt:lt + 1])
                    eng.activation(s10[:], x2t, AF.Square,
                                   accum_out=rms_rs[:, lt:lt + 1])
                pl = lts[3]
                eng.wait_ge(sV.h, v10_0 + 15 * 3 + 4)
                eng.activation(q_r[:, pl:pl + 1], q_r[:, pl:pl + 1],
                               AF.Ln, bias=eps_col[:])
                eng.activation(q_r[:, pl:pl + 1], q_r[:, pl:pl + 1],
                               AF.Exp, scale=-0.5)
            S(p10_scalar)

            def p10_vec(eng, HF=HF, lts=lts, s10_0=s10_0):
                for i, lt in enumerate(lts):
                    x2t = x2[:, lt * HID:(lt + 1) * HID]
                    eng.wait_ge(sS.h, s10_0 + 4 * i + 4)
                    eng.tensor_scalar(q_m[:, lt:lt + 1], rms_ms[:, lt:lt + 1],
                                      1.0 / HID, None, ALU.mult)
                    eng.tensor_scalar(q_r[:, lt:lt + 1], rms_rs[:, lt:lt + 1],
                                      1.0 / HID, None, ALU.mult)
                    eng.tensor_tensor(q_am[:, lt:lt + 1], q_m[:, lt:lt + 1],
                                      q_m[:, lt:lt + 1], ALU.mult)
                    eng.tensor_sub(q_r[:, lt:lt + 1], q_r[:, lt:lt + 1],
                                   q_am[:, lt:lt + 1])
                    eng.tensor_scalar(s10[:], x2t, q_m[:, lt:lt + 1], None,
                                      ALU.subtract)
                    eng.reduce_max(q_am[:, lt:lt + 1], s10[:], AX,
                                   apply_absolute_value=True)
                    eng.wait_ge(sS.h, lnexp_cnt(i))
                    eng.tensor_tensor(q_am[:, lt:lt + 1], q_am[:, lt:lt + 1],
                                      q_r[:, lt:lt + 1], ALU.mult)
                    eng.tensor_scalar_max(q_am[:, lt:lt + 1],
                                          q_am[:, lt:lt + 1], EPS)
                    eng.reciprocal(q_sa[:, lt:lt + 1], q_am[:, lt:lt + 1])
                    eng.tensor_scalar(q_sa[:, lt:lt + 1], q_sa[:, lt:lt + 1],
                                      127.0, None, ALU.mult)
                    eng.tensor_tensor(fq1[:, lt:lt + 1], q_r[:, lt:lt + 1],
                                      q_sa[:, lt:lt + 1], ALU.mult)
                    eng.tensor_scalar(s10[:], s10[:], fq1[:, lt:lt + 1], None,
                                      ALU.mult)
                    eng.tensor_scalar(xqT[:, i * HID:(i + 1) * HID], s10[:],
                                      MAGIC, MAGIC, ALU.add, ALU.subtract)
                    eng.tensor_tensor(fq1[:, lt:lt + 1], q_am[:, lt:lt + 1],
                                      qthr[:, 6:7], ALU.mult)
                    eng.tensor_tensor(fq3[:, lt:lt + 1], q_am[:, lt:lt + 1],
                                      qthr[:, 7:8], ALU.mult)
            V(p10_vec)
            barrier()

            t11_0 = sT.n; s11_0 = sS.n

            def p11_tensor(eng, t11_0=t11_0, s11_0=s11_0):
                for i in range(4):
                    for kh in range(2):
                        idx = i * 2 + kh
                        eng.fence()
                        if idx >= 2:
                            eng.wait_ge(sS.h, s11_0 + idx - 1)
                        pv = pbig_b[:, (idx % 2) * 1024:((idx % 2) + 1) * 1024]
                        for kk in range(8):
                            k = kh * 8 + kk
                            eng.transpose(
                                pv[:, kk * 128:(kk + 1) * 128],
                                xqT[:, i * HID + k * 128:i * HID + (k + 1) * 128],
                                identb[:])
            TE(p11_tensor)

            def p11_scalar(eng, t11_0=t11_0):
                for i in range(4):
                    for kh in range(2):
                        idx = i * 2 + kh
                        eng.wait_ge(sT.h, t11_0 + 8 * (idx + 1))
                        k0 = kh * 8
                        eng.copy(
                            xqF[:].rearrange("p (k t) -> p k t", k=16)
                            [:, k0:k0 + 8, i * 128:(i + 1) * 128],
                            pbig_b[:, (idx % 2) * 1024:((idx % 2) + 1) * 1024]
                            .rearrange("p (k t) -> p k t", k=8))
            S(p11_scalar)
            barrier()

            build_bcast([fq1[:, lt:lt + 1] for lt in lts], RS1[:])
            barrier()

            t12_0 = sT.n; v12_0 = sV.n; s12_0 = sS.n
            LA, LB = sLdA.n, sLdB.n
            STb = sSt.n
            vb0 = v12_0 + 2

            if HF == 0:
                CA0, CB0 = sCastA.n, sCastB.n
                g12_0 = sG.n
                vtmp = vb(7680, 512)

                def silu_cnt(j, s0=s12_0):
                    return s0 + 5 * (j + 1) + 5 if j < NHT - 1 else s0 + 5 * NHT + 1

                def p12b_sync(eng, g0=g12_0, v0=vb0, s0=s12_0, ST0=STb,
                              LA=LA, LB=LB):
                    for ht in range(NHT):
                        for jj in range(2):
                            if ht >= 1:
                                eng.wait_ge(sV.h, v0 + 10 * (ht - 1) + 3 * (jj + 1))
                            (sLdA, sLdB)[jj].dma(eng.dma_start(
                                out=wring4[jj][:],
                                in_=I["w1t"][ht].rearrange("i k j -> i (k j)")
                                [:, jj * 1024:(jj + 1) * 1024]))
                        if ht >= 1:
                            eng.wait_ge(sSt.h, ST0 + 32 * ht)
                        eng.wait_ge(sV.h, v0 + 10 * ht + 6)
                        sSt.dma(eng.dma_start(out=th1[ht], in_=t1bR[ht % 2][:]))
                        eng.wait_ge(sG.h, g0 + 2 * ht + 2)
                        sSt.dma(eng.dma_start(out=th3[ht], in_=t3bR[ht % 2][:]))
                SY(p12b_sync)

                def p12b_gp(eng, g0=g12_0, s0=s12_0, t0=t12_0, ST0=STb):
                    for ht in range(NHT):
                        for jj in range(2):
                            if ht >= 1:
                                eng.wait_ge(sS.h, s0 + 5 * (ht - 1) + 2 * (jj + 1))
                            (sCastA, sCastB)[jj].dma(eng.dma_start(
                                out=wring4[2 + jj][:],
                                in_=I["w3t"][ht].rearrange("i k j -> i (k j)")
                                [:, jj * 1024:(jj + 1) * 1024]))
                        for half in range(2):
                            eng.wait_ge(sS.h, s0 + 5 * ht + 2 * (half + 1))
                            if half == 0 and ht >= 2:
                                eng.wait_ge(sT.h, t0 + 2 * (ht - 2) + 2)
                                eng.wait_ge(sSt.h, ST0 + 32 * (ht - 1))
                            eng.tensor_add(
                                t3bR[ht % 2][:, half * 1024:(half + 1) * 1024],
                                sg4[2 * half][:], sg4[2 * half + 1][:])
                G(p12b_gp)

                def p12b_scalar(eng, g0=g12_0, s0=s12_0, v0=vb0,
                                CA0=CA0, CB0=CB0):
                    for ht in range(NHT):
                        for half in range(2):
                            eng.wait_ge((sCastA, sCastB)[half].h,
                                        (CA0, CB0)[half] + 16 * (ht + 1))
                            if ht >= 1:
                                eng.wait_ge(sG.h, g0 + 2 * (ht - 1) + half + 1)
                            w = wring4[2 + half]
                            eng.activation(sg4[2 * half][:], w[:], AF.Sign,
                                           bias=qthr[:, 5:6])
                            eng.activation(sg4[2 * half + 1][:], w[:],
                                           AF.Sign, bias=qthr[:, 2:3])
                        if ht >= 1:
                            eng.wait_ge(sV.h, v0 + 10 * ht + 7)
                            eng.activation(scrF2[:], scrF[:], AF.Silu)
                        else:
                            eng.pad(1)
                    eng.wait_ge(sV.h, v0 + 10 * NHT + 1)
                    eng.activation(scrF2[:], scrF[:], AF.Silu)
                S(p12b_scalar)

                def p12b_vec(eng, v0=vb0, s0=s12_0, t0=t12_0, ST0=STb,
                             LA=LA, LB=LB):
                    eng.memset(amax_acc[:], 0.0)
                    eng.memset(amin_acc[:], 0.0)

                    def evac(ht):
                        b1 = (pA0, pB0)[ht % 2]; b3 = (pA1, pB1)[ht % 2]
                        eng.wait_ge(sT.h, t0 + 2 * ht + 1)
                        if ht >= 1:
                            eng.wait_ge(sS.h, silu_cnt(ht - 1))
                        eng.tensor_tensor(scrF[:], b1[:], RS1[:], ALU.mult)
                        eng.wait_ge(sS.h, silu_cnt(ht))
                        eng.wait_ge(sT.h, t0 + 2 * ht + 2)
                        eng.tensor_tensor(hbuf[:, ht * 512:(ht + 1) * 512],
                                          scrF2[:], b3[:], ALU.mult)
                        eng.tensor_max(amax_acc[:], amax_acc[:],
                                       hbuf[:, ht * 512:(ht + 1) * 512])
                        eng.tensor_tensor(amin_acc[:], amin_acc[:],
                                          hbuf[:, ht * 512:(ht + 1) * 512],
                                          ALU.min)

                    for ht in range(NHT):
                        for jj in range(2):
                            eng.wait_ge((sLdA, sLdB)[jj].h,
                                        (LA, LB)[jj] + 16 * (ht + 1))
                            if jj == 0 and ht >= 2:
                                eng.wait_ge(sT.h, t0 + 2 * (ht - 2) + 1)
                                eng.wait_ge(sSt.h, ST0 + 32 * (ht - 1))
                            w = wring4[jj]
                            hsl = slice(jj * 1024, (jj + 1) * 1024)
                            eng.tensor_scalar(t1bR[ht % 2][:, hsl], w[:],
                                              qthr[:, 0:1], None, ALU.is_gt)
                            eng.tensor_scalar(vtmp[:], w[:], qthr[:, 3:4],
                                              None, ALU.is_lt)
                            eng.tensor_sub(t1bR[ht % 2][:, hsl],
                                           t1bR[ht % 2][:, hsl], vtmp[:])
                        if ht >= 1:
                            evac(ht - 1)
                        else:
                            eng.pad(4)
                    evac(NHT - 1)
                V(p12b_vec)

                def p12b_tensor(eng, g0=g12_0, v0=vb0):
                    for ht in range(NHT):
                        b1 = (pA0, pB0)[ht % 2]; b3 = (pA1, pB1)[ht % 2]
                        eng.fence()
                        eng.wait_ge(sV.h, v0 + 10 * ht + 6)
                        if ht >= 2:
                            eng.wait_ge(sV.h, v0 + 10 * (ht - 1) + 7)
                        for k in range(16):
                            eng.matmul(b1[:],
                                       t1bR[ht % 2][:, k * 128:(k + 1) * 128],
                                       xqF[:, k * 512:(k + 1) * 512],
                                       start=(k == 0), stop=(k == 15))
                        eng.fence()
                        eng.wait_ge(sG.h, g0 + 2 * ht + 2)
                        if ht >= 2:
                            eng.wait_ge(sV.h, v0 + 10 * (ht - 1) + 8)
                        for k in range(16):
                            eng.matmul(b3[:],
                                       t3bR[ht % 2][:, k * 128:(k + 1) * 128],
                                       xqF[:, k * 512:(k + 1) * 512],
                                       start=(k == 0), stop=(k == 15))
                TE(p12b_tensor)
            else:
                stp2 = sSt.n
                CA1, CB1 = sCastA.n, sCastB.n

                def p12b_sync(eng, t0=t12_0, stp2=stp2):
                    eng.wait_ge(sSt.h, stp2)
                    for ht in range(NHT):
                        if ht >= 2:
                            eng.wait_ge(sT.h, t0 + 2 * (ht - 2) + 1)
                        (sLdA, sLdB)[ht % 2].dma(
                            eng.dma_start(out=t1bR[ht % 2][:], in_=th1[ht]))
                SY(p12b_sync)

                def p12b_gp1(eng, t0=t12_0, stp2=stp2):
                    eng.wait_ge(sSt.h, stp2)
                    for ht in range(NHT):
                        if ht >= 2:
                            eng.wait_ge(sT.h, t0 + 2 * (ht - 2) + 2)
                        (sCastA, sCastB)[ht % 2].dma(
                            eng.dma_start(out=t3bR[ht % 2][:], in_=th3[ht]))
                G(p12b_gp1)

                def p12b_vec(eng, v0=vb0, s0=s12_0, t0=t12_0):
                    eng.memset(amax_acc[:], 0.0)
                    eng.memset(amin_acc[:], 0.0)
                    for ht in range(NHT):
                        b1 = (pA0, pB0)[ht % 2]; b3 = (pA1, pB1)[ht % 2]
                        eng.wait_ge(sT.h, t0 + 2 * ht + 1)
                        if ht >= 1:
                            eng.wait_ge(sS.h, s0 + ht)
                        eng.tensor_tensor(scrF[:], b1[:], RS1[:], ALU.mult)
                        eng.wait_ge(sS.h, s0 + ht + 1)
                        eng.wait_ge(sT.h, t0 + 2 * ht + 2)
                        eng.tensor_tensor(hbuf[:, ht * 512:(ht + 1) * 512],
                                          scrF2[:], b3[:], ALU.mult)
                        eng.tensor_max(amax_acc[:], amax_acc[:],
                                       hbuf[:, ht * 512:(ht + 1) * 512])
                        eng.tensor_tensor(amin_acc[:], amin_acc[:],
                                          hbuf[:, ht * 512:(ht + 1) * 512],
                                          ALU.min)
                V(p12b_vec)

                def p12b_scalar(eng, v0=vb0):
                    for ht in range(NHT):
                        eng.wait_ge(sV.h, v0 + 4 * ht + 1)
                        eng.activation(scrF2[:], scrF[:], AF.Silu)
                S(p12b_scalar)

                def p12b_tensor(eng, v0=vb0, LA=LA, LB=LB,
                                CA1=CA1, CB1=CB1):
                    for ht in range(NHT):
                        L1 = t1bR[ht % 2]
                        L3 = t3bR[ht % 2]
                        b1 = (pA0, pB0)[ht % 2]; b3 = (pA1, pB1)[ht % 2]
                        eng.fence()
                        eng.wait_ge((sLdA, sLdB)[ht % 2].h,
                                    (LA, LB)[ht % 2] + 16 * (ht // 2 + 1))
                        if ht >= 2:
                            eng.wait_ge(sV.h, v0 + 4 * (ht - 2) + 1)
                        for k in range(16):
                            eng.matmul(b1[:], L1[:, k * 128:(k + 1) * 128],
                                       xqF[:, k * 512:(k + 1) * 512],
                                       start=(k == 0), stop=(k == 15))
                        eng.fence()
                        eng.wait_ge((sCastA, sCastB)[ht % 2].h,
                                    (CA1, CB1)[ht % 2] + 16 * (ht // 2 + 1))
                        if ht >= 2:
                            eng.wait_ge(sV.h, v0 + 4 * (ht - 2) + 2)
                        for k in range(16):
                            eng.matmul(b3[:], L3[:, k * 128:(k + 1) * 128],
                                       xqF[:, k * 512:(k + 1) * 512],
                                       start=(k == 0), stop=(k == 15))
                TE(p12b_tensor)
            barrier()

            tc0 = sT.n

            def p12c_tensor(eng):
                eng.fence()
                for i in range(4):
                    eng.transpose(pbig[:, i * 128:(i + 1) * 128],
                                  amax_acc[:, i * 128:(i + 1) * 128], identf[:])
                    eng.transpose(pbig[:, 512 + i * 128:512 + (i + 1) * 128],
                                  amin_acc[:, i * 128:(i + 1) * 128], identf[:])
                eng.fence()
            TE(p12c_tensor)
            tc1 = sT.n

            def p12c_vec(eng, lts=lts, tc1=tc1):
                eng.wait_ge(sT.h, tc1)
                for i, lt in enumerate(lts):
                    eng.tensor_scalar(qv_scr[:, 0:128],
                                      pbig[:, 512 + i * 128:512 + (i + 1) * 128],
                                      -1.0, None, ALU.mult)
                    eng.tensor_max(qv_scr[:, 0:128], qv_scr[:, 0:128],
                                   pbig[:, i * 128:(i + 1) * 128])
                    eng.reduce_max(rms_ms[:, lt:lt + 1],
                                   qv_scr[:, 0:128], AX)
                    eng.tensor_tensor(rms_rs[:, lt:lt + 1], rms_ms[:, lt:lt + 1],
                                      fq3[:, lt:lt + 1], ALU.mult)
                    eng.tensor_scalar_max(rms_rs[:, lt:lt + 1],
                                          rms_rs[:, lt:lt + 1], EPS)
                    eng.reciprocal(q_sa[:, lt:lt + 1], rms_rs[:, lt:lt + 1])
                    eng.tensor_scalar(q_sa[:, lt:lt + 1], q_sa[:, lt:lt + 1],
                                      127.0, None, ALU.mult)
                    eng.tensor_tensor(fq1[:, lt:lt + 1], fq3[:, lt:lt + 1],
                                      q_sa[:, lt:lt + 1], ALU.mult)
                    eng.tensor_tensor(fq3[:, lt:lt + 1], rms_rs[:, lt:lt + 1],
                                      qthr[:, 8:9], ALU.mult)
            V(p12c_vec)

            build_bcast([fq1[:, lt:lt + 1] for lt in lts], SCLH[:])
            build_bcast([fq3[:, lt:lt + 1] for lt in lts], RS2[:])

            def p12c_quant(eng):
                for cq in range(11):
                    eng.tensor_tensor(
                        qv_scr[:].rearrange("p (a t) -> p a t", a=4),
                        hbuf[:, cq * 2048:(cq + 1) * 2048]
                        .rearrange("p (a t) -> p a t", a=4),
                        SCLH[:][:, None, :].broadcast_to([128, 4, 512]),
                        ALU.mult)
                    eng.tensor_scalar(hbuf[:, cq * 2048:(cq + 1) * 2048],
                                      qv_scr[:], MAGIC, MAGIC,
                                      ALU.add, ALU.subtract)
            V(p12c_quant)
            barrier()

            t12d = sT.n; v12d = sV.n; s12d = sS.n
            LA2, LB2 = sLdA.n, sLdB.n
            CA2 = sCastA.n
            STd = sSt.n
            ksplit = [(0, 16), (16, 32), (32, 44)]

            if HF == 1:
                vA = sV.n

                def pstoreA(eng, vA=vA):
                    eng.wait_ge(sV.h, vA)
                    for lt in range(4):
                        sSt.dma(eng.dma_start(
                            out=out_h[lt * 128:(lt + 1) * 128, :],
                            in_=x2[:, lt * HID:(lt + 1) * HID]))
                SY(pstoreA)

            if HF == 0:
                gd0 = sG.n
                CA0, CB0 = sCastA.n, sCastB.n
                vtmpD = vb(OFF_A, 1024)
                sgnw4 = [vb(OFF_A + 1024, 1024), vb(OFF_A + 2048, 1024),
                         vb(OFF_A + 3072, 1024), vb(7168, 1024)]
                wbJ = [vf(OFF_D + 1024 + j * 2048, 2048) for j in range(3)]
                KSL = [(0, 2048), (2048, 4096), (4096, 5632)]

                def p12d_sync(eng, g0=gd0, v0=v12d, ST0=STd):
                    eng.wait_ge(sSt.h, ST0)
                    for d in range(16):
                        if d >= 1:
                            eng.wait_ge(sV.h, v0 + 5 * (d - 1) + 2)
                        sLdA.dma(eng.dma_start(
                            out=wbJ[0][:],
                            in_=I["w2t"][d].rearrange("i k j -> i (k j)")
                            [:, 0:2048]))
                        if d >= 1:
                            eng.wait_ge(sSt.h, ST0 + 16 * d)
                        eng.wait_ge(sV.h, v0 + 5 * d + 3)
                        eng.wait_ge(sG.h, g0 + 2 * d + 2)
                        sSt.dma(eng.dma_start(out=th2[d], in_=w2tR[d % 2][:]))
                SY(p12d_sync)

                def p12d_gp(eng, g0=gd0, s0=s12d, t0=t12d, ST0=STd):
                    for d in range(16):
                        for jp in range(2):
                            if d >= 1:
                                eng.wait_ge(sS.h, s0 + 4 * (d - 1) + 2 * (jp + 1))
                            (sCastA, sCastB)[jp].dma(eng.dma_start(
                                out=wbJ[1 + jp][0:128, 0:KSL[1 + jp][1]
                                                - KSL[1 + jp][0]],
                                in_=I["w2t"][d].rearrange("i k j -> i (k j)")
                                [:, KSL[1 + jp][0]:KSL[1 + jp][1]]))
                        for jp in range(2):
                            eng.wait_ge(sS.h, s0 + 4 * d + 2 * (jp + 1))
                            if jp == 0:
                                if d >= 2:
                                    eng.wait_ge(sT.h, t0 + 5 * (d - 2) + 1)
                                eng.wait_ge(sSt.h,
                                            ST0 + (16 * (d - 1) if d >= 2
                                                   else 0))
                            a, b = KSL[1 + jp]
                            eng.tensor_add(w2tR[d % 2][:, a:b],
                                           sgnw4[2 * jp][0:128, 0:b - a],
                                           sgnw4[2 * jp + 1][0:128, 0:b - a])
                G(p12d_gp)

                def p12d_scalar(eng, g0=gd0, s0=s12d, CA0=CA0, CB0=CB0):
                    for d in range(16):
                        for jp in range(2):
                            eng.wait_ge((sCastA, sCastB)[jp].h,
                                        (CA0, CB0)[jp] + 16 * (d + 1))
                            if d >= 1:
                                eng.wait_ge(sG.h, g0 + 2 * (d - 1) + jp + 1)
                            a, b = KSL[1 + jp]
                            w = wbJ[1 + jp][0:128, 0:b - a]
                            eng.activation(sgnw4[2 * jp][0:128, 0:b - a], w,
                                           AF.Sign, bias=qthr[:, 4:5])
                            eng.activation(sgnw4[2 * jp + 1][0:128, 0:b - a],
                                           w, AF.Sign, bias=qthr[:, 1:2])
                S(p12d_scalar)

                def p12d_vec(eng, v0=v12d, s0=s12d, t0=t12d, ST0=STd,
                             LA2=LA2):
                    def evac(d):
                        eng.wait_ge(sT.h, t0 + 5 * d + 1)
                        eng.tensor_tensor(ffn_scr[:],
                                          (pA0, pB0)[d % 2][:], RS2[:],
                                          ALU.mult)
                        eng.wait_ge(sT.h,
                                    t0 + (5 * (d + 1) + 5 if d < 15 else 84))
                        eng.tensor_add(
                            x2[:].rearrange("p (l f) -> p l f", l=NLT)
                            [:, HF * 4:HF * 4 + 4, d * 128:(d + 1) * 128],
                            x2[:].rearrange("p (l f) -> p l f", l=NLT)
                            [:, HF * 4:HF * 4 + 4, d * 128:(d + 1) * 128],
                            pbig[:, 0:512].rearrange("p (l f) -> p l f", l=4))

                    for d in range(16):
                        eng.wait_ge(sLdA.h, LA2 + 16 * (d + 1))
                        if d >= 2:
                            eng.wait_ge(sT.h, t0 + 5 * (d - 2) + 1)
                        eng.wait_ge(sSt.h,
                                    ST0 + (16 * (d - 1) if d >= 2 else 0))
                        eng.tensor_scalar(w2tR[d % 2][:, 0:2048], wbJ[0][:],
                                          qthr[:, 1:2], 2.0, ALU.is_gt,
                                          ALU.mult)
                        eng.tensor_scalar(vtmpD[:], wbJ[0][:], qthr[:, 4:5],
                                          2.0, ALU.is_lt, ALU.mult)
                        eng.tensor_sub(w2tR[d % 2][:, 0:2048],
                                       w2tR[d % 2][:, 0:2048], vtmpD[:])
                        if d >= 1:
                            evac(d - 1)
                        else:
                            eng.pad(2)
                    evac(15)
                V(p12d_vec)

                def p12d_tensor(eng, g0=gd0, v0=v12d):
                    for d in range(16):
                        bank = (pA0, pB0)[d % 2]
                        eng.fence()
                        eng.wait_ge(sV.h, v0 + 5 * d + 3)
                        eng.wait_ge(sG.h, g0 + 2 * d + 2)
                        if d >= 2:
                            eng.wait_ge(sV.h, v0 + 5 * (d - 1) + 4)
                        for k in range(NHT):
                            eng.matmul(bank[:],
                                       w2tR[d % 2][:, k * 128:(k + 1) * 128],
                                       hbuf[:, k * 512:(k + 1) * 512],
                                       start=(k == 0), stop=(k == NHT - 1))
                        eng.fence()
                        if d >= 1:
                            eng.wait_ge(sV.h, v0 + 5 * d + 4)
                            for i in range(4):
                                eng.transpose(
                                    pbig[:, i * 128:(i + 1) * 128],
                                    ffn_scr[:, i * 128:(i + 1) * 128],
                                    identf[:])
                        else:
                            eng.pad(4)
                    eng.fence()
                    eng.wait_ge(sV.h, v0 + 5 * 16 + 1)
                    for i in range(4):
                        eng.transpose(pbig[:, i * 128:(i + 1) * 128],
                                      ffn_scr[:, i * 128:(i + 1) * 128],
                                      identf[:])
                TE(p12d_tensor)
            else:
                def p12d_sync(eng, t0=t12d):
                    for d in range(16):
                        wt = w2tR[d % 2]
                        if d >= 2:
                            eng.wait_ge(sT.h, t0 + 5 * (d - 2) + 1)
                        sem = (sLdA, sLdB)[d % 2]
                        sem.dma(eng.dma_start(out=wt[:, 0:4096],
                                              in_=th2[d][:, 0:4096]))
                        sem.dma(eng.dma_start(out=wt[:, 4096:5632],
                                              in_=th2[d][:, 4096:5632]))
                SY(p12d_sync)

                def p12d_vec(eng, v0=v12d, t0=t12d):
                    for d in range(16):
                        eng.wait_ge(sT.h, t0 + 5 * d + 1)
                        eng.tensor_tensor(ffn_scr[:],
                                          (pA0, pB0)[d % 2][:], RS2[:],
                                          ALU.mult)
                        eng.wait_ge(sT.h, t0 + 5 * d + 5)
                        eng.tensor_add(
                            x2[:].rearrange("p (l f) -> p l f", l=NLT)
                            [:, HF * 4:HF * 4 + 4, d * 128:(d + 1) * 128],
                            x2[:].rearrange("p (l f) -> p l f", l=NLT)
                            [:, HF * 4:HF * 4 + 4, d * 128:(d + 1) * 128],
                            pbig[:, 0:512].rearrange("p (l f) -> p l f", l=4))
                V(p12d_vec)

                def p12d_tensor(eng, v0=v12d, LA2=LA2, LB2=LB2):
                    for d in range(16):
                        wt = w2tR[d % 2]
                        bank = (pA0, pB0)[d % 2]
                        eng.fence()
                        eng.wait_ge((sLdA, sLdB)[d % 2].h,
                                    (LA2, LB2)[d % 2] + 32 * (d // 2 + 1))
                        if d >= 2:
                            eng.wait_ge(sV.h, v0 + 2 * (d - 2) + 1)
                        for k in range(NHT):
                            eng.matmul(bank[:],
                                       wt[:, k * 128:(k + 1) * 128],
                                       hbuf[:, k * 512:(k + 1) * 512],
                                       start=(k == 0), stop=(k == NHT - 1))
                        eng.fence()
                        eng.wait_ge(sV.h, v0 + 2 * d + 1)
                        for i in range(4):
                            eng.transpose(pbig[:, i * 128:(i + 1) * 128],
                                          ffn_scr[:, i * 128:(i + 1) * 128],
                                          identf[:])
                TE(p12d_tensor)
            barrier()

        fv = sV.n

        def fin_sync(eng):
            eng.wait_ge(sV.h, fv)
            for lt in range(4, NLT):
                sSt.dma(eng.dma_start(out=out_h[lt * 128:(lt + 1) * 128, :],
                                      in_=x2[:, lt * HID:(lt + 1) * HID]))
            eng.wait_ge(sSt.h, sSt.n)
        SY(fin_sync)

    return nc


def kernel(**inputs):
    shared = _prep_shared(inputs)
    maps = [_prep_core(inputs, shared, c) for c in range(NCORE)]
    nc = build_nc()
    res = run_bass_kernel_spmd(nc, maps, core_ids=list(range(NCORE)))
    out = np.concatenate([res.results[c]["out"] for c in range(NCORE)], 0)
    return out[None]
